# version 21
# baseline (speedup 1.0000x reference)
import numpy as np
import concourse.bass as bass
import concourse.mybir as mybir
from concourse.bass_utils import run_bass_kernel_spmd

F32 = mybir.dt.float32
BF16 = mybir.dt.bfloat16
I32 = mybir.dt.int32
AF = mybir.ActivationFunctionType
ALU = mybir.AluOpType
AX = mybir.AxisListType

P = 128
D = 1024
S = 4096
E = 16
DC = D // P
NT = S // P
CAP = 768
GT_N = CAP // P
BIGC = 65536.0
ARENA_W = 52736

ENGINES = ("sync", "tensor", "vector", "scalar", "gpsimd")


class Prog:

    def __init__(self, nc):
        self.nc = nc
        self.ops = {e: [] for e in ENGINES}
        self._cms = []

    def sem(self, name):
        cm = self.nc.semaphore(name)
        h = cm.__enter__()
        self._cms.append(cm)

        class Sem:
            def __init__(s):
                s.h = h
                s.n = 0
                s.mark = 0
        return Sem()

    def emit(self, eng, fn, waits=(), inc=None, amt=1, drain=True):
        ws = tuple((s.h, v) for (s, v) in waits if v > 0)
        self.ops[eng].append((ws, fn, inc.h if inc is not None else None, amt, drain))
        if inc is not None:
            inc.n += amt
            return (inc, inc.n)
        return None

    def dma(self, eng, out_ap, in_ap, waits=(), inc=None, slow_ok=False):
        def fn(e, o=out_ap, i=in_ap, s=slow_ok):
            return e.dma_start(out=o, in_=i, allow_slow_non_contiguous=s)
        ws = list(waits)
        if inc is not None:
            ws.append((inc, inc.mark))
        return self.emit(eng, fn, waits=ws, inc=inc, amt=16)

    @staticmethod
    def burst(sem):
        sem.mark = sem.n

    def replay(self, eng_name, eng):
        drains = eng_name in ("vector", "scalar")
        first = True
        waited = {}
        for ws, fn, inc_h, amt, drain in self.ops[eng_name]:
            if drains and drain and not first:
                eng.drain()
            first = False
            for h, v in ws:
                if waited.get(h, -1) < v:
                    eng.wait_ge(h, v)
                    waited[h] = v
            instr = fn(eng)
            if inc_h is not None:
                instr.then_inc(inc_h, amt)


def build(phases=("p1", "pw", "p2", "p3"), debug_outputs=False):
    nc = bass.Bass("TRN2", target_bir_lowering=False)

    x_h = nc.dram_tensor("x", [S, D], F32, kind="ExternalInput").ap()
    wq_h = nc.dram_tensor("wq", [D, D], F32, kind="ExternalInput").ap()
    wk_h = nc.dram_tensor("wk", [D, D], F32, kind="ExternalInput").ap()
    wv_h = nc.dram_tensor("wv", [D, D], F32, kind="ExternalInput").ap()
    gw_h = nc.dram_tensor("gw", [E, D], F32, kind="ExternalInput").ap()
    ew_h = nc.dram_tensor("ew", [E, D, D], F32, kind="ExternalInput").ap()
    eb_h = nc.dram_tensor("eb", [E, D], F32, kind="ExternalInput").ap()
    id_h = nc.dram_tensor("ident", [P, P], F32, kind="ExternalInput").ap()
    on_h = nc.dram_tensor("onesr", [1, P], F32, kind="ExternalInput").ap()
    lt_h = nc.dram_tensor("ltri", [P, P], F32, kind="ExternalInput").ap()
    oc_h = nc.dram_tensor("onesc", [P, 1], F32, kind="ExternalInput").ap()
    ie_h = nc.dram_tensor("iotaE", [P, E], F32, kind="ExternalInput").ap()
    ib_h = nc.dram_tensor("iotaEB", [P, E], F32, kind="ExternalInput").ap()
    i128_h = nc.dram_tensor("iota128", [P, 1], F32, kind="ExternalInput").ap()

    y_h = nc.dram_tensor("y", [S, D], F32, kind="ExternalOutput").ap()

    sk = "ExternalOutput" if debug_outputs else "Internal"
    qTh_h = nc.dram_tensor("qTh", [D, S], BF16, kind=sk).ap()
    qTl_h = nc.dram_tensor("qTl", [D, S], BF16, kind=sk).ap()
    kTh_h = nc.dram_tensor("kTh", [D, S], BF16, kind=sk).ap()
    kTl_h = nc.dram_tensor("kTl", [D, S], BF16, kind=sk).ap()
    vh_h = nc.dram_tensor("vh", [S, D], BF16, kind=sk).ap()
    vl_h = nc.dram_tensor("vl", [S, D], BF16, kind=sk).ap()
    wTh_h = nc.dram_tensor("wTh", [E, D, D], BF16, kind="Internal").ap()
    wTl_h = nc.dram_tensor("wTl", [E, D, D], BF16, kind="Internal").ap()
    x2_h = nc.dram_tensor("x2", [S, D], F32, kind=sk).ap()
    tlnT_h = nc.dram_tensor("tlnT", [D, S], F32, kind=sk).ap()
    tln_h = nc.dram_tensor("tln", [S, D], F32, kind="Internal").ap()
    toks_h = nc.dram_tensor("toks", [E * CAP], I32, kind="Internal").ap()
    ws_h = nc.dram_tensor("ws", [E * CAP], F32, kind="Internal").ap()
    hE_h = nc.dram_tensor("hE", [E * CAP, D], F32, kind="Internal").ap()
    adr_h = nc.dram_tensor("adr", [S, 2], I32, kind="Internal").ap()

    arena_cm = nc.sbuf_tensor("arena", [P, ARENA_W], F32)
    AR = arena_cm.__enter__()
    ARB = AR.bitcast(BF16)
    off = [0]

    def alloc(w):
        o = off[0]
        assert o + w <= ARENA_W, f"arena overflow at {o}+{w}"
        off[0] = o + w
        return o

    def view(o, w, p0=0, pn=P):
        return AR[p0:p0 + pn, o:o + w]

    def bpair(w):
        o = alloc(w)
        return (ARB[:, 2 * o:2 * o + w], ARB[:, 2 * o + w:2 * o + 2 * w])

    o_ident = alloc(128)
    o_ones = alloc(128)
    o_cols = alloc(128)
    o_gwT = alloc(128)
    o_ltri = alloc(128)
    o_onesc = alloc(1)
    o_iotaE = alloc(E)
    o_iotaEB = alloc(E)
    o_i128 = alloc(1)
    o_rows = alloc(64)
    off[0] = (off[0] + 127) // 128 * 128
    o_base = off[0]

    ident = view(o_ident, 128)
    ones_r = view(o_ones, 128, 0, 1)
    gwT = view(o_gwT, 128)
    ltri = view(o_ltri, 128)
    onesc = view(o_onesc, 1)
    iotaE = view(o_iotaE, E)
    iotaEB = view(o_iotaEB, E)
    i128c = view(o_i128, 1)

    def rows(i, n=1):
        return view(o_rows + i, n, 0, 1)

    def cols(i, n=1):
        return view(o_cols + i, n)

    def region():
        off[0] = o_base

    ix_cm = nc.sbuf_tensor("ixbuf", [P, 28], I32)
    IX = ix_cm.__enter__()

    pb_cms = [nc.psum_tensor(f"pb{i}", [P, 512], F32) for i in range(8)]
    pb = [cm.__enter__() for cm in pb_cms]

    pr = Prog(nc)
    q_ld = pr.sem("q_ld")
    q_st = pr.sem("q_st")
    q_ix = pr.sem("q_ix")
    q_sc = pr.sem("q_sc")
    s_t = pr.sem("s_t")
    s_v = pr.sem("s_v")
    s_s = pr.sem("s_s")
    allsems = (q_ld, q_st, q_ix, q_sc, s_t, s_v, s_s)

    def snap():
        return [(s, s.n) for s in allsems if s.n > 0]

    bfree = {i: [] for i in range(8)}

    def mm3(b, shi, slo, mhi, mlo, start, stop, waits=(), inc=None):
        pr.emit("tensor", lambda t_, b=b, shi=shi, mhi=mhi, start=start: t_.matmul(
            pb[b][:, :], shi, mhi, start=start, stop=False), waits=waits)
        pr.emit("tensor", lambda t_, b=b, shi=shi, mlo=mlo: t_.matmul(
            pb[b][:, :], shi, mlo, start=False, stop=False))
        return pr.emit("tensor", lambda t_, b=b, slo=slo, mhi=mhi, stop=stop: t_.matmul(
            pb[b][:, :], slo, mhi, start=False, stop=stop), inc=inc)

    def split_psum(b, hi_ap, lo_ap, scale, hi_waits=(), lo_waits=()):
        pr.emit("scalar", lambda s, b=b, hi_ap=hi_ap, scale=scale: s.activation(
            hi_ap, pb[b][:, :], AF.Copy, scale=scale),
            waits=list(hi_waits), inc=s_s)
        pr.emit("vector", lambda v, b=b, hi_ap=hi_ap, lo_ap=lo_ap, scale=scale: v.scalar_tensor_tensor(
            lo_ap, pb[b][:, :], scale, hi_ap, ALU.mult, ALU.subtract),
            waits=[(s_s, s_s.n)] + list(lo_waits), inc=s_v, drain=False)
        return (s_v, s_v.n)

    def split_sbuf(src_ap, hi_ap, lo_ap, waits=()):
        pr.emit("scalar", lambda s, src_ap=src_ap, hi_ap=hi_ap: s.activation(
            hi_ap, src_ap, AF.Copy), waits=list(waits), inc=s_s)
        pr.emit("vector", lambda v, src_ap=src_ap, hi_ap=hi_ap, lo_ap=lo_ap: v.tensor_tensor(
            lo_ap, src_ap, hi_ap, ALU.subtract),
            waits=[(s_s, s_s.n)], inc=s_v, drain=False)
        return (s_v, s_v.n)

    pr.dma("sync", ident, id_h, inc=q_ld)
    pr.dma("sync", ones_r, on_h, inc=q_ld)
    pr.dma("sync", ltri, lt_h, inc=q_ld)
    pr.dma("sync", onesc, oc_h, inc=q_ld)
    pr.dma("sync", iotaE, ie_h, inc=q_ld)
    pr.dma("sync", iotaEB, ib_h, inc=q_ld)
    pr.dma("sync", i128c, i128_h, inc=q_ld)
    region()
    wqp = bpair(DC * D)
    wkp = bpair(DC * D)
    wvp = bpair(DC * D)
    wstg = [view(alloc(D), D) for _ in range(2)]
    xbuf = view(alloc(4 * D), 4 * D)
    xln = view(alloc(4 * D), 4 * D)
    xlnTp = bpair(DC * 512)
    qTbp = bpair(4 * D)
    kTbp = bpair(4 * D)
    o_vbufp = off[0]
    vbufp = bpair(2 * D)
    gwsrc = view(o_vbufp, D, 0, E)
    ewpp = [bpair(D) for _ in range(2)]

    wstg_free = [[], []]
    for mi, (src_h, pair) in enumerate(((wq_h, wqp), (wk_h, wkp), (wv_h, wvp))):
        for c in range(DC):
            par = (mi * DC + c) % 2
            pr.burst(q_ld)
            pr.dma("sync", wstg[par], src_h[c * P:(c + 1) * P, :],
                   waits=wstg_free[par], inc=q_ld)
            tok = split_sbuf(wstg[par], pair[0][:, c * D:(c + 1) * D],
                             pair[1][:, c * D:(c + 1) * D],
                             waits=[(q_ld, q_ld.n)])
            wstg_free[par] = [tok]
    w_split_done = (s_v, s_v.n)

    pr.burst(q_ld)
    pr.dma("sync", gwsrc, gw_h, inc=q_ld)
    ld0 = (q_ld, q_ld.n)

    for c in range(DC):
        pr.emit("tensor",
                lambda t, c=c: t.transpose(pb[0][:, c * E:(c + 1) * E],
                                           gwsrc[:, c * P:(c + 1) * P],
                                           ident[0:E, 0:E]),
                waits=[ld0], inc=s_t if c == DC - 1 else None)
    pr.emit("vector", lambda v: v.tensor_copy(gwT, pb[0][:, 0:P]),
            waits=[(s_t, s_t.n)], inc=s_v)
    gwT_ready = (s_v, s_v.n)
    bfree[0] = [gwT_ready]

    def ln_tiles(xsrc, xdst, ntiles, wait_ready, extra_apply_waits=()):
        for t in range(ntiles):
            sl = (slice(None), slice(t * D, (t + 1) * D))
            pr.emit("vector", lambda v, t=t, sl=sl: v.reduce_sum(
                cols(0 + t), xsrc[sl], axis=AX.X), waits=[wait_ready])
            pr.emit("vector", lambda v, t=t: v.tensor_scalar(
                cols(0 + t), cols(0 + t), 1.0 / D, None, ALU.mult), inc=s_v)
        meang = (s_v, s_v.n)
        for t in range(ntiles):
            sl = (slice(None), slice(t * D, (t + 1) * D))
            pr.emit("vector", lambda v, t=t, sl=sl: v.scalar_tensor_tensor(
                xdst[sl], xsrc[sl], cols(0 + t), xsrc[sl], ALU.subtract, ALU.mult,
                accum_out=cols(4 + t)), waits=[meang], inc=s_v)
        for t in range(ntiles):
            pr.emit("vector", lambda v, t=t: v.tensor_scalar(
                cols(4 + t), cols(4 + t), 1.0 / D, 1e-5, ALU.mult, ALU.add),
                inc=s_v)
        varg = (s_v, s_v.n)
        for t in range(ntiles):
            pr.emit("scalar", lambda s, t=t: s.activation(
                cols(8 + t), cols(4 + t), AF.Sqrt),
                waits=[varg], inc=s_s)
        sqg = (s_s, s_s.n)
        for t in range(ntiles):
            pr.emit("vector", lambda v, t=t: v.reciprocal(
                cols(12 + t), cols(8 + t)), waits=[sqg], inc=s_v)
        vg = (s_v, s_v.n)
        for t in range(ntiles):
            sl = (slice(None), slice(t * D, (t + 1) * D))
            pr.emit("vector", lambda v, t=t, sl=sl: v.tensor_scalar(
                xdst[sl], xsrc[sl], cols(0 + t), cols(12 + t),
                ALU.subtract, ALU.mult),
                waits=[vg] + list(extra_apply_waits), inc=s_v)

    if "p1" in phases:
        xbuf_free = []
        xlnT_wfree = []
        qTb_free, kTb_free = [], []
        vfree = [[], []]
        ewpp_free = [[], []]
        pr.burst(q_ld)
        pr.dma("sync", xbuf.rearrange("p (t d) -> p t d", t=4),
               x_h[0:512, :].rearrange("(t p) d -> p t d", p=P), inc=q_ld)
        xld_tok = (q_ld, q_ld.n)
        for nb in range(8):
            ln_tiles(xbuf, xln, 4, xld_tok)
            xln_ready = (s_v, s_v.n)
            xbuf_free = [(s_v, s_v.n)]
            for c in range(DC):
                b = c % 2
                for t in range(4):
                    pr.emit("tensor", lambda t_, c=c, t2=t, b=b: t_.transpose(
                        pb[b][:, t2 * P:(t2 + 1) * P],
                        xln[:, t2 * D + c * P: t2 * D + (c + 1) * P], ident),
                        waits=[xln_ready] + bfree[b],
                        inc=s_t if t == 3 else None)
                tok = split_psum(b, xlnTp[0][:, c * 512:(c + 1) * 512],
                                 xlnTp[1][:, c * 512:(c + 1) * 512], 1.0,
                                 hi_waits=[(s_t, s_t.n)] + xlnT_wfree,
                                 lo_waits=xlnT_wfree)
                bfree[b] = [tok]
            xlnT_ready = tok
            for proj in ("q", "k"):
                wpair = wqp if proj == "q" else wkp
                blkp = qTbp if proj == "q" else kTbp
                scale = (1.0 / 32.0) if proj == "q" else 1.0
                free_tok = qTb_free if proj == "q" else kTb_free
                for ec in range(DC):
                    b = 2 + ec % 2
                    for c in range(DC):
                        mm3(b, wpair[0][:, c * D + ec * P: c * D + (ec + 1) * P],
                            wpair[1][:, c * D + ec * P: c * D + (ec + 1) * P],
                            xlnTp[0][:, c * 512:(c + 1) * 512],
                            xlnTp[1][:, c * 512:(c + 1) * 512],
                            start=(c == 0), stop=(c == DC - 1),
                            waits=([xlnT_ready, w_split_done] + bfree[b]) if c == 0 else (),
                            inc=s_t if c == DC - 1 else None)
                    tok = split_psum(b, blkp[0][:, ec * 512:(ec + 1) * 512],
                                     blkp[1][:, ec * 512:(ec + 1) * 512], scale,
                                     hi_waits=[(s_t, s_t.n)] + free_tok,
                                     lo_waits=free_tok)
                    bfree[b] = [tok]
                sc_done = tok
                hdst = qTh_h if proj == "q" else kTh_h
                ldst = qTl_h if proj == "q" else kTl_h
                pr.burst(q_st)
                pr.dma("gpsimd", hdst.rearrange("(c p) n -> p c n", p=P)[:, :, nb * 512:(nb + 1) * 512],
                       blkp[0].rearrange("p (c n) -> p c n", c=DC), waits=[sc_done], inc=q_st)
                pr.dma("gpsimd", ldst.rearrange("(c p) n -> p c n", p=P)[:, :, nb * 512:(nb + 1) * 512],
                       blkp[1].rearrange("p (c n) -> p c n", c=DC), waits=[sc_done], inc=q_st)
                if proj == "q":
                    qTb_free = [(q_st, q_st.n)]
                else:
                    kTb_free = [(q_st, q_st.n)]
            for t in range(4):
                ts = t % 2
                for eh in range(2):
                    b = 4 + (t * 2 + eh) % 2
                    for c in range(DC):
                        mm3(b, xlnTp[0][:, c * 512 + t * P: c * 512 + (t + 1) * P],
                            xlnTp[1][:, c * 512 + t * P: c * 512 + (t + 1) * P],
                            wvp[0][:, c * D + eh * 512: c * D + (eh + 1) * 512],
                            wvp[1][:, c * D + eh * 512: c * D + (eh + 1) * 512],
                            start=(c == 0), stop=(c == DC - 1),
                            waits=([xlnT_ready, w_split_done] + bfree[b]) if c == 0 else (),
                            inc=s_t if c == DC - 1 else None)
                    tok = split_psum(b, vbufp[0][:, ts * D + eh * 512: ts * D + (eh + 1) * 512],
                                     vbufp[1][:, ts * D + eh * 512: ts * D + (eh + 1) * 512], 1.0,
                                     hi_waits=[(s_t, s_t.n)] + vfree[ts],
                                     lo_waits=vfree[ts])
                    bfree[b] = [tok]
                pr.burst(q_st)
                pr.dma("gpsimd", vh_h[(nb * 4 + t) * P:(nb * 4 + t + 1) * P, :],
                       vbufp[0][:, ts * D:(ts + 1) * D], waits=[tok], inc=q_st)
                pr.dma("gpsimd", vl_h[(nb * 4 + t) * P:(nb * 4 + t + 1) * P, :],
                       vbufp[1][:, ts * D:(ts + 1) * D], waits=[tok], inc=q_st)
                vfree[ts] = [(q_st, q_st.n)]
            xlnT_wfree = [(s_t, s_t.n)]
            if nb < 7:
                pr.burst(q_ld)
                pr.dma("sync", xbuf.rearrange("p (t d) -> p t d", t=4),
                       x_h[(nb + 1) * 512:(nb + 2) * 512, :].rearrange(
                           "(t p) d -> p t d", p=P),
                       waits=xbuf_free, inc=q_ld)
                xld_tok = (q_ld, q_ld.n)
            for ce in range(16):
                e = nb * 2 + ce // 8
                c = ce % 8
                par = ce % 2
                pr.burst(q_ld)
                pr.dma("sync", wstg[par], ew_h[e][c * P:(c + 1) * P, :],
                       waits=wstg_free[par], inc=q_ld)
                tok = split_sbuf(wstg[par], ewpp[par][0], ewpp[par][1],
                                 waits=[(q_ld, q_ld.n)] + ewpp_free[par])
                wstg_free[par] = [tok]
                pr.burst(q_st)
                pr.dma("gpsimd", wTh_h[e][c * P:(c + 1) * P, :], ewpp[par][0],
                       waits=[tok], inc=q_st)
                pr.dma("gpsimd", wTl_h[e][c * P:(c + 1) * P, :], ewpp[par][1],
                       waits=[tok], inc=q_st)
                ewpp_free[par] = [(q_st, q_st.n)]

    p1_end = snap()
    pw_end = p1_end

    if "p2" in phases:
        region()
        kTgp = [bpair(2048) for _ in range(2)]
        qTb2p = [bpair(4096) for _ in range(2)]
        vb2p = [bpair(4 * D) for _ in range(2)]
        PTbp = [bpair(4096) for _ in range(2)]
        Sb = [view(alloc(4 * D), 4 * D) for _ in range(2)]
        Pb = [view(alloc(4 * D), 4 * D) for _ in range(2)]
        pmax = [view(alloc(16), 16) for _ in range(2)]
        xres = view(alloc(D), D)
        x2t = view(alloc(D), D)
        x2ln = view(alloc(D), D)
        tlnTb = view(alloc(D), D)

        kTg_free = [pw_end, pw_end]
        qTb2_free = [pw_end, pw_end]
        vb2_free = [pw_end, pw_end]
        P_free = [pw_end, pw_end]
        PT_free = [pw_end, pw_end]
        x2t_free = []
        x2ln_free = []
        tlnTb_free = []
        for i in range(8):
            bfree[i] = bfree[i] + pw_end

        def ln_one(xsrc, xdst, wait_ready, extra_apply_waits=()):
            pr.emit("vector", lambda v: v.reduce_sum(cols(0), xsrc, axis=AX.X),
                    waits=[wait_ready])
            pr.emit("vector", lambda v: v.tensor_scalar(
                cols(0), cols(0), 1.0 / D, None, ALU.mult), inc=s_v)
            g1 = (s_v, s_v.n)
            pr.emit("vector", lambda v: v.scalar_tensor_tensor(
                xdst, xsrc, cols(0), xsrc, ALU.subtract, ALU.mult,
                accum_out=cols(1)), waits=[g1] + list(extra_apply_waits), inc=s_v)
            pr.emit("vector", lambda v: v.tensor_scalar(
                cols(1), cols(1), 1.0 / D, 1e-5, ALU.mult, ALU.add), inc=s_v)
            pr.emit("scalar", lambda s: s.activation(
                cols(2), cols(1), AF.Sqrt),
                waits=[(s_v, s_v.n)], inc=s_s)
            pr.emit("vector", lambda v: v.reciprocal(cols(3), cols(2)),
                    waits=[(s_s, s_s.n)], inc=s_v)
            g2 = (s_v, s_v.n)
            pr.emit("vector", lambda v: v.tensor_scalar(
                xdst, xsrc, cols(0), cols(3), ALU.subtract, ALU.mult),
                waits=[g2] + list(extra_apply_waits), inc=s_v)

        for jg in range(16):
            kpar = jg % 2
            pr.burst(q_ld)
            pr.dma("sync", kTgp[kpar][0].rearrange("p (c n) -> p c n", c=DC),
                   kTh_h.rearrange("(c p) n -> p c n", p=P)[:, :, jg * 256:(jg + 1) * 256],
                   waits=kTg_free[kpar], inc=q_ld)
            pr.dma("sync", kTgp[kpar][1].rearrange("p (c n) -> p c n", c=DC),
                   kTl_h.rearrange("(c p) n -> p c n", p=P)[:, :, jg * 256:(jg + 1) * 256],
                   waits=kTg_free[kpar], inc=q_ld)
            kTg_ready = (q_ld, q_ld.n)
            for ib in range(8):
                par = ib % 2
                pr.burst(q_ld)
                pr.dma("sync", qTb2p[par][0].rearrange("p (c n) -> p c n", c=DC),
                       qTh_h.rearrange("(c p) n -> p c n", p=P)[:, :, ib * 512:(ib + 1) * 512],
                       waits=qTb2_free[par], inc=q_ld)
                pr.dma("sync", qTb2p[par][1].rearrange("p (c n) -> p c n", c=DC),
                       qTl_h.rearrange("(c p) n -> p c n", p=P)[:, :, ib * 512:(ib + 1) * 512],
                       waits=qTb2_free[par], inc=q_ld)
                q_ready = (q_ld, q_ld.n)
                for jt in range(2):
                    b = (ib * 2 + jt) % 2
                    for ec in range(DC):
                        mm3(b, kTgp[kpar][0][:, ec * 256 + jt * P: ec * 256 + (jt + 1) * P],
                            kTgp[kpar][1][:, ec * 256 + jt * P: ec * 256 + (jt + 1) * P],
                            qTb2p[par][0][:, ec * 512:(ec + 1) * 512],
                            qTb2p[par][1][:, ec * 512:(ec + 1) * 512],
                            start=(ec == 0), stop=(ec == DC - 1),
                            waits=([kTg_ready, q_ready] + bfree[b]) if ec == 0 else (),
                            inc=s_t if ec == DC - 1 else None)
                    pr.emit("scalar", lambda s, ib=ib, jt=jt, b=b: s.activation(
                        Sb[jt][:, ib * 512:(ib + 1) * 512], pb[b][:, :], AF.Copy, scale=1.0),
                        waits=[(s_t, s_t.n)], inc=s_s)
                    bfree[b] = [(s_s, s_s.n)]
                    pr.emit("vector", lambda v, ib=ib, jt=jt: v.reduce_max(
                        pmax[jt][:, ib:ib + 1], Sb[jt][:, ib * 512:(ib + 1) * 512],
                        axis=AX.X), waits=[(s_s, s_s.n)], inc=s_v, drain=False)
                qTb2_free[par] = [(s_t, s_t.n)]
            s_done = (s_s, s_s.n)
            exp_tok = [None, None]
            for jt in range(2):
                pr.emit("vector", lambda v, jt=jt: v.reduce_max(
                    cols(16 + jt), pmax[jt], axis=AX.X, negate=True),
                    waits=[(s_v, s_v.n)], inc=s_v)
                pr.emit("scalar", lambda s, jt=jt: s.activation(
                    Pb[jt], Sb[jt], AF.Exp, bias=cols(16 + jt),
                    accum_out=cols(18 + jt)),
                    waits=[(s_v, s_v.n)] + P_free[jt], inc=s_s)
                exp_tok[jt] = (s_s, s_s.n)
                pr.emit("vector", lambda v, jt=jt: v.reciprocal(
                    cols(20 + jt), cols(18 + jt)), waits=[(s_s, s_s.n)], inc=s_v)
            inv_ready = (s_v, s_v.n)
            sp_tok = [[None] * 8, [None] * 8]

            def emit_pt(g):
                for jt in range(2):
                    b = 2 + (g * 2 + jt) % 2
                    for k in range(4):
                        ic = g * 4 + k
                        pr.emit("tensor", lambda t_, ic=ic, k=k, jt=jt, b=b: t_.transpose(
                            pb[b][:, k * P:(k + 1) * P],
                            Pb[jt][:, ic * P:(ic + 1) * P], ident),
                            waits=[exp_tok[jt]] + bfree[b],
                            inc=s_t if k == 3 else None)
                    tok = split_psum(b, PTbp[jt][0][:, g * 512:(g + 1) * 512],
                                     PTbp[jt][1][:, g * 512:(g + 1) * 512], 1.0,
                                     hi_waits=[(s_t, s_t.n)] + PT_free[jt],
                                     lo_waits=PT_free[jt])
                    sp_tok[jt][g] = tok
                    bfree[b] = [tok]

            emit_pt(0)
            for vb in range(8):
                if vb + 1 < 8:
                    emit_pt(vb + 1)
                par = vb % 2
                pr.burst(q_ld)
                pr.dma("sync", vb2p[par][0].rearrange("p (cc d) -> p cc d", cc=4),
                       vh_h[vb * 512:(vb + 1) * 512, :].rearrange("(cc p) d -> p cc d", p=P),
                       waits=vb2_free[par], inc=q_ld)
                pr.dma("sync", vb2p[par][1].rearrange("p (cc d) -> p cc d", cc=4),
                       vl_h[vb * 512:(vb + 1) * 512, :].rearrange("(cc p) d -> p cc d", p=P),
                       waits=vb2_free[par], inc=q_ld)
                v_ready = (q_ld, q_ld.n)
                for cc in range(4):
                    ic = vb * 4 + cc
                    for jt in range(2):
                        for eh in range(2):
                            b = 4 + jt * 2 + eh
                            last = (cc == 3 and jt == 1 and eh == 1)
                            mm3(b, PTbp[jt][0][:, ic * P:(ic + 1) * P],
                                PTbp[jt][1][:, ic * P:(ic + 1) * P],
                                vb2p[par][0][:, cc * D + eh * 512: cc * D + (eh + 1) * 512],
                                vb2p[par][1][:, cc * D + eh * 512: cc * D + (eh + 1) * 512],
                                start=(ic == 0), stop=(ic == 31),
                                waits=[sp_tok[0][vb], sp_tok[1][vb], v_ready]
                                      + (bfree[b] if ic == 0 else []),
                                inc=s_t if last else None)
                vb2_free[par] = [(s_t, s_t.n)]
            for jt in range(2):
                P_free[jt] = [(s_t, s_t.n)]
            av_done = (s_t, s_t.n)
            PT_free = [[av_done], [av_done]]
            for jt in range(2):
                jr = jg * 2 + jt
                pr.burst(q_ld)
                pr.dma("sync", xres, x_h[jr * P:(jr + 1) * P, :],
                       waits=x2t_free, inc=q_ld)
                xr_ready = (q_ld, q_ld.n)
                for eh in range(2):
                    b = 4 + jt * 2 + eh
                    pr.emit("vector", lambda v, jt=jt, eh=eh, b=b: v.scalar_tensor_tensor(
                        x2t[:, eh * 512:(eh + 1) * 512], pb[b][:, :], cols(20 + jt),
                        xres[:, eh * 512:(eh + 1) * 512], ALU.mult, ALU.add),
                        waits=[av_done, xr_ready, inv_ready] + x2t_free, inc=s_v,
                        drain=False)
                    bfree[b] = [(s_v, s_v.n)]
                x2_ready = (s_v, s_v.n)
                pr.burst(q_st)
                pr.dma("gpsimd", x2_h[jr * P:(jr + 1) * P, :], x2t,
                       waits=[x2_ready], inc=q_st)
                ln_one(x2t, x2ln, x2_ready, extra_apply_waits=x2ln_free)
                ln2_done = (s_v, s_v.n)
                x2t_free = [(s_v, s_v.n), (q_st, q_st.n)]
                pr.burst(q_st)
                pr.dma("gpsimd", tln_h[jr * P:(jr + 1) * P, :], x2ln,
                       waits=[ln2_done], inc=q_st)
                tln_store_tok = (q_st, q_st.n)
                for half in range(2):
                    b = 2 + half
                    for k in range(4):
                        c = half * 4 + k
                        pr.emit("tensor", lambda t_, c=c, k=k, b=b: t_.transpose(
                            pb[b][:, k * P:(k + 1) * P],
                            x2ln[:, c * P:(c + 1) * P], ident),
                            waits=[ln2_done] + bfree[b],
                            inc=s_t if k == 3 else None)
                    pr.emit("vector", lambda v, half=half, b=b: v.tensor_copy(
                        tlnTb[:, half * 512:(half + 1) * 512], pb[b][:, :]),
                        waits=[(s_t, s_t.n)] + tlnTb_free, inc=s_v, drain=False)
                    bfree[b] = [(s_v, s_v.n)]
                x2ln_free = [(s_t, s_t.n), tln_store_tok]
                tb_ready = (s_v, s_v.n)
                pr.burst(q_st)
                pr.dma("gpsimd", tlnT_h.rearrange("(c p) n -> p c n", p=P)[:, :, jr * P:(jr + 1) * P],
                       tlnTb.rearrange("p (c n) -> p c n", c=DC), waits=[tb_ready], inc=q_st)
                tlnTb_free = [(q_st, q_st.n)]
            kTg_free[kpar] = [(s_t, s_t.n)]

    p2_end = snap()

    if "p3" in phases:
        region()
        tln = view(alloc(DC * D), DC * D)
        lgall = view(alloc(P), P)
        pall = view(alloc(P), P)
        cndall = view(alloc(P), P)
        geall = view(alloc(P), P)
        ohall1 = view(alloc(P), P)
        ohall2 = view(alloc(P), P)
        t8all = view(alloc(64), 64)
        sta = view(alloc(64), 64)
        pmd = view(alloc(E), E)
        zsb = view(alloc(E * CAP // P), E * CAP // P)

        p3a_start = p2_end
        slab_free = p3a_start

        pr.emit("vector", lambda v: v.memset(zsb, 0.0), waits=p3a_start, inc=s_v)
        zs_ready = (s_v, s_v.n)
        pr.burst(q_ix)
        pr.dma("gpsimd", toks_h.rearrange("(p g) -> p g", p=P), zsb,
               waits=[zs_ready], inc=q_ix)
        pr.dma("gpsimd", ws_h.rearrange("(p g) -> p g", p=P), zsb,
               waits=[zs_ready], inc=q_ix)
        zs_done = (q_ix, q_ix.n)

        pr.emit("vector", lambda v: v.memset(rows(0, 64), 0.0), inc=s_v)
        base_tok = (s_v, s_v.n)

        scat_done_prev = zs_done
        adr_done_prev = (q_st, 0)
        for slab in range(4):
            pr.burst(q_ld)
            pr.dma("sync", tln.rearrange("p (c n) -> p c n", c=DC),
                   tlnT_h.rearrange("(c p) n -> p c n", p=P)[:, :, slab * D:(slab + 1) * D],
                   waits=slab_free, inc=q_ld)
            slab_ready = (q_ld, q_ld.n)
            for ntile in range(8):
                for c in range(DC):
                    pr.emit("tensor", lambda t_, c=c, ntile=ntile: t_.matmul(
                        pb[0][:, ntile * E:(ntile + 1) * E],
                        tln[:, c * D + ntile * P: c * D + (ntile + 1) * P],
                        gwT[:, c * E:(c + 1) * E],
                        start=(c == 0), stop=(c == DC - 1)),
                        waits=([slab_ready, gwT_ready] + bfree[0]) if (ntile == 0 and c == 0) else (),
                        inc=s_t if (ntile == 7 and c == DC - 1) else None)
            pr.emit("scalar", lambda s: s.activation(
                lgall, pb[0][:, 0:P], AF.Copy, scale=1.0),
                waits=[(s_t, s_t.n)], inc=s_s)
            lg_tok = (s_s, s_s.n)
            bfree[0] = [lg_tok]
            for t in range(8):
                pr.emit("vector", lambda v, t=t: v.reduce_max(
                    sta[:, 0 + t:1 + t], lgall[:, t * E:(t + 1) * E], axis=AX.X,
                    negate=True),
                    waits=[lg_tok, scat_done_prev, adr_done_prev], inc=s_v)
            gmx_tok = (s_v, s_v.n)
            for t in range(8):
                pr.emit("scalar", lambda s, t=t: s.activation(
                    pall[:, t * E:(t + 1) * E], lgall[:, t * E:(t + 1) * E], AF.Exp,
                    bias=sta[:, 0 + t:1 + t], accum_out=sta[:, 8 + t:9 + t]),
                    waits=[gmx_tok], inc=s_s)
            exp_tok2 = (s_s, s_s.n)
            for t in range(8):
                pr.emit("vector", lambda v, t=t: v.reciprocal(
                    sta[:, 16 + t:17 + t], sta[:, 8 + t:9 + t]),
                    waits=[exp_tok2], inc=s_v)
            rec_tok = (s_v, s_v.n)
            for t in range(8):
                pr.emit("vector", lambda v, t=t: v.tensor_scalar(
                    pall[:, t * E:(t + 1) * E], pall[:, t * E:(t + 1) * E],
                    sta[:, 16 + t:17 + t], None, ALU.mult),
                    waits=[rec_tok], inc=s_v)
            for t in range(8):
                pr.emit("vector", lambda v, t=t: v.max(
                    t8all[:, t * 8:(t + 1) * 8], pall[:, t * E:(t + 1) * E]),
                    inc=s_v)
            t8_tok = (s_v, s_v.n)
            for t in range(8):
                pr.emit("vector", lambda v, t=t: v.tensor_scalar(
                    geall[:, t * E:(t + 1) * E], pall[:, t * E:(t + 1) * E],
                    t8all[:, t * 8:t * 8 + 1], None, ALU.is_ge),
                    waits=[t8_tok], inc=s_v)
            for t in range(8):
                pr.emit("vector", lambda v, t=t: v.scalar_tensor_tensor(
                    cndall[:, t * E:(t + 1) * E], geall[:, t * E:(t + 1) * E],
                    -BIGC, iotaEB, ALU.mult, ALU.add), inc=s_v)
            for t in range(8):
                pr.emit("vector", lambda v, t=t: v.tensor_reduce(
                    sta[:, 24 + t:25 + t], cndall[:, t * E:(t + 1) * E],
                    axis=AX.X, op=ALU.min), inc=s_v)
            e1_tok = (s_v, s_v.n)
            for t in range(8):
                pr.emit("vector", lambda v, t=t: v.tensor_scalar(
                    ohall1[:, t * E:(t + 1) * E], iotaE, sta[:, 24 + t:25 + t],
                    None, ALU.is_equal), waits=[e1_tok], inc=s_v)
            for t in range(8):
                pr.emit("vector", lambda v, t=t: v.tensor_scalar(
                    geall[:, t * E:(t + 1) * E], pall[:, t * E:(t + 1) * E],
                    t8all[:, t * 8 + 1:t * 8 + 2], None, ALU.is_ge), inc=s_v)
            for t in range(8):
                pr.emit("vector", lambda v, t=t: v.scalar_tensor_tensor(
                    cndall[:, t * E:(t + 1) * E], geall[:, t * E:(t + 1) * E],
                    -BIGC, iotaEB, ALU.mult, ALU.add), inc=s_v)
            oh1_tok = (s_v, s_v.n)
            for t in range(8):
                pr.emit("vector", lambda v, t=t: v.scalar_tensor_tensor(
                    cndall[:, t * E:(t + 1) * E], ohall1[:, t * E:(t + 1) * E],
                    BIGC, cndall[:, t * E:(t + 1) * E], ALU.mult, ALU.add),
                    waits=[oh1_tok], inc=s_v)
            for t in range(8):
                pr.emit("vector", lambda v, t=t: v.tensor_reduce(
                    sta[:, 32 + t:33 + t], cndall[:, t * E:(t + 1) * E],
                    axis=AX.X, op=ALU.min), inc=s_v)
            e2_tok = (s_v, s_v.n)
            for t in range(8):
                pr.emit("vector", lambda v, t=t: v.tensor_scalar(
                    ohall2[:, t * E:(t + 1) * E], iotaE, sta[:, 32 + t:33 + t],
                    None, ALU.is_equal), waits=[e2_tok], inc=s_v)
            oh_all_tok = (s_v, s_v.n)
            for ntile in range(8):
                gt = slab * 8 + ntile
                oh1 = ohall1[:, ntile * E:(ntile + 1) * E]
                oh2 = ohall2[:, ntile * E:(ntile + 1) * E]
                pr.emit("tensor", lambda t_, oh1=oh1: t_.matmul(
                    pb[1][:, 0:E], ltri, oh1, start=True, stop=False),
                    waits=[oh_all_tok, base_tok] + bfree[1] + bfree[2] + bfree[3])
                pr.emit("tensor", lambda t_: t_.matmul(
                    pb[1][:, 0:E], ones_r, rows(0, E), start=False, stop=True))
                pr.emit("tensor", lambda t_, oh1=oh1: t_.matmul(
                    pb[2][0:1, 0:E], onesc, oh1, start=True, stop=True))
                pr.emit("tensor", lambda t_, oh2=oh2: t_.matmul(
                    pb[3][:, 0:E], ltri, oh2, start=True, stop=False),
                    inc=s_t)
                mm1 = (s_t, s_t.n)
                pr.emit("vector", lambda v, oh1=oh1: v.scalar_tensor_tensor(
                    pmd, pb[1][:, 0:E], 1.0, oh1, ALU.mult, ALU.mult,
                    accum_out=cols(29)), waits=[mm1], inc=s_v)
                pr.emit("vector", lambda v: v.tensor_tensor(
                    rows(16, E), rows(0, E), pb[2][0:1, 0:E], ALU.add), inc=s_v)
                base_b_tok = (s_v, s_v.n)
                pr.emit("tensor", lambda t_: t_.matmul(
                    pb[3][:, 0:E], ones_r, rows(16, E), start=False, stop=True),
                    waits=[base_b_tok])
                pr.emit("tensor", lambda t_, oh2=oh2: t_.matmul(
                    pb[2][0:1, 16:32], onesc, oh2, start=True, stop=True),
                    inc=s_t)
                mm2 = (s_t, s_t.n)
                pr.emit("vector", lambda v, oh2=oh2: v.scalar_tensor_tensor(
                    pmd, pb[3][:, 0:E], 1.0, oh2, ALU.mult, ALU.mult,
                    accum_out=cols(30)), waits=[mm2], inc=s_v)
                pr.emit("vector", lambda v: v.tensor_tensor(
                    rows(0, E), rows(16, E), pb[2][0:1, 16:32], ALU.add), inc=s_v)
                base_tok = (s_v, s_v.n)
                bfree[1] = [base_tok]
                bfree[2] = [base_tok]
                bfree[3] = [base_tok]
                g4 = (s_v, s_v.n)
                pr.emit("vector", lambda v, ntile=ntile: v.tensor_scalar(
                    cols(72), sta[:, 24 + ntile:25 + ntile], float(CAP), cols(29),
                    ALU.mult, ALU.add), waits=[g4], inc=s_v)
                pr.emit("vector", lambda v, ntile=ntile: v.tensor_scalar(
                    cols(74), sta[:, 24 + ntile:25 + ntile], float(CAP),
                    float(CAP - 1), ALU.mult, ALU.add), inc=s_v)
                pr.emit("vector", lambda v: v.tensor_tensor(
                    cols(72), cols(72), cols(74), ALU.min), inc=s_v)
                pr.emit("vector", lambda v, ntile=ntile: v.tensor_scalar(
                    cols(73), sta[:, 32 + ntile:33 + ntile], float(CAP), cols(30),
                    ALU.mult, ALU.add), inc=s_v)
                pr.emit("vector", lambda v, ntile=ntile: v.tensor_scalar(
                    cols(74), sta[:, 32 + ntile:33 + ntile], float(CAP),
                    float(CAP - 1), ALU.mult, ALU.add), inc=s_v)
                pr.emit("vector", lambda v: v.tensor_tensor(
                    cols(73), cols(73), cols(74), ALU.min), inc=s_v)
                pr.emit("vector", lambda v, gt=gt: v.tensor_scalar(
                    cols(75), i128c, float(gt * P), None, ALU.add), inc=s_v)
                pr.emit("vector", lambda v: v.tensor_copy(IX[:, 24:25], cols(72)),
                        waits=[scat_done_prev, adr_done_prev], inc=s_v)
                pr.emit("vector", lambda v: v.tensor_copy(IX[:, 25:26], cols(73)), inc=s_v)
                pr.emit("vector", lambda v: v.tensor_copy(IX[:, 26:27], cols(75)), inc=s_v)
                route_ready = (s_v, s_v.n)
                pr.burst(q_st)
                pr.dma("gpsimd", adr_h[gt * P:(gt + 1) * P, :], IX[:, 24:26],
                       waits=[route_ready], inc=q_st)
                adr_done_prev = (q_st, q_st.n)
                pr.burst(q_ix)
                mark = (q_ix, q_ix.mark)
                toks2 = toks_h.rearrange("(a b) -> a b", b=1)
                ws2 = ws_h.rearrange("(a b) -> a b", b=1)
                for (ixc, vals) in ((24, IX[:, 26:27]), (25, IX[:, 26:27])):
                    pr.emit("gpsimd", lambda g, ixc=ixc, vals=vals, t2=toks2: g.indirect_dma_start(
                        t2, bass.IndirectOffsetOnAxis(ap=IX[:, ixc:ixc + 1], axis=0),
                        vals, None),
                        waits=[route_ready, mark], inc=q_ix, amt=16)
                for (ixc, tt) in ((24, 0), (25, 1)):
                    wv_ap = t8all[:, ntile * 8 + tt: ntile * 8 + tt + 1]
                    pr.emit("gpsimd", lambda g, ixc=ixc, wv_ap=wv_ap, w2=ws2: g.indirect_dma_start(
                        w2, bass.IndirectOffsetOnAxis(ap=IX[:, ixc:ixc + 1], axis=0),
                        wv_ap, None),
                        waits=[route_ready, mark], inc=q_ix, amt=16)
                scat_done_prev = (q_ix, q_ix.n)
            slab_free = [(s_t, s_t.n)]
        routing_done = (q_ix, q_ix.n)
        p3a_end = snap()

        region()
        Gb = [view(alloc(GT_N * D), GT_N * D) for _ in range(2)]
        GTp = bpair(DC * CAP)
        GT3h = GTp[0].rearrange("p (c n) -> p c n", c=DC)
        GT3l = GTp[1].rearrange("p (c n) -> p c n", c=DC)
        wtp = [bpair(DC * D) for _ in range(2)]
        wc = [view(alloc(8), 8) for _ in range(2)]
        ebr = [view(alloc(D), D, 0, 1) for _ in range(2)]
        bbc = [view(alloc(D), D) for _ in range(2)]
        t0b = [view(alloc(512), 512) for _ in range(2)]
        tE = [view(alloc(D), D) for _ in range(2)]

        wt_free = [p3a_end, p3a_end]
        eb_free = [p3a_end, p3a_end]
        wc_free = [p3a_end, p3a_end]
        idx_free = [p3a_end, p3a_end]
        bbc_free = [p3a_end, p3a_end]
        G_free = [p3a_end, p3a_end]
        GT_free = []
        t0_free = [[], []]
        tE_free = [[], []]
        w_ready = [None] * E
        g_ready = [None] * E

        def p3_loads(e):
            par = e % 2
            pr.burst(q_ld)
            for c in range(DC):
                pr.emit("sync", lambda s, e=e, c=c, par=par: s.dma_start(
                    out=wtp[par][0][:, c * D:(c + 1) * D],
                    in_=wTh_h[e][:, c * P:(c + 1) * P], transpose=True),
                    waits=wt_free[par] + [(q_ld, q_ld.mark)], inc=q_ld, amt=16)
                pr.emit("sync", lambda s, e=e, c=c, par=par: s.dma_start(
                    out=wtp[par][1][:, c * D:(c + 1) * D],
                    in_=wTl_h[e][:, c * P:(c + 1) * P], transpose=True),
                    waits=wt_free[par] + [(q_ld, q_ld.mark)], inc=q_ld, amt=16)
            pr.dma("sync", ebr[par], eb_h[e:e + 1, :], waits=eb_free[par], inc=q_ld)
            pr.dma("sync", wc[par][:, 0:GT_N],
                   ws_h[e * CAP:(e + 1) * CAP].rearrange("(g p) -> p g", p=P),
                   waits=wc_free[par] + [routing_done], inc=q_ld, slow_ok=True)
            pr.dma("sync", IX[:, par * 6 + 0: par * 6 + GT_N],
                   toks_h[e * CAP:(e + 1) * CAP].rearrange("(g p) -> p g", p=P),
                   waits=wc_free[par] + idx_free[par] + [routing_done], inc=q_ld,
                   slow_ok=True)
            w_ready[e] = (q_ld, q_ld.n)

        def p3_gathers(e):
            par = e % 2
            pr.burst(q_ix)
            for g in range(GT_N):
                pr.emit("gpsimd", lambda gp, g=g, par=par: gp.indirect_dma_start(
                    Gb[par][:, g * D:(g + 1) * D], None, tln_h,
                    bass.IndirectOffsetOnAxis(
                        ap=IX[:, par * 6 + g: par * 6 + g + 1], axis=0)),
                    waits=[w_ready[e], routing_done, (q_ix, q_ix.mark)] + G_free[par],
                    inc=q_ix, amt=16)
            g_ready[e] = (q_ix, q_ix.n)

        p3_loads(0)
        p3_gathers(0)
        for e in range(E):
            par = e % 2
            if e + 1 < E:
                p3_loads(e + 1)
                p3_gathers(e + 1)
            for eh in range(2):
                pr.emit("tensor", lambda t_, eh=eh, par=par: t_.matmul(
                    pb[4 + eh][:, :], ones_r, ebr[par][:, eh * 512:(eh + 1) * 512],
                    start=True, stop=True),
                    waits=[w_ready[e]] + bfree[4 + eh],
                    inc=s_t if eh == 1 else None)
            for eh in range(2):
                pr.emit("vector", lambda v, eh=eh, par=par: v.tensor_copy(
                    bbc[par][:, eh * 512:(eh + 1) * 512], pb[4 + eh][:, :]),
                    waits=[(s_t, s_t.n)] + bbc_free[par], inc=s_v, drain=False)
                bfree[4 + eh] = [(s_v, s_v.n)]
            bb_ready = (s_v, s_v.n)
            gt_tok = [None] * GT_N

            def emit_gt(g, par=par, e=e):
                for half in range(2):
                    b = 6 + (g * 2 + half) % 2
                    for k in range(4):
                        c = half * 4 + k
                        pr.emit("tensor", lambda t_, g=g, c=c, k=k, par=par, b=b: t_.transpose(
                            pb[b][:, k * P:(k + 1) * P],
                            Gb[par][:, g * D + c * P: g * D + (c + 1) * P], ident),
                            waits=[g_ready[e]] + bfree[b],
                            inc=s_t if k == 3 else None)
                    tok = split_psum(b, GT3h[:, half * 4:(half + 1) * 4, g * P:(g + 1) * P],
                                     GT3l[:, half * 4:(half + 1) * 4, g * P:(g + 1) * P], 1.0,
                                     hi_waits=[(s_t, s_t.n)] + GT_free,
                                     lo_waits=GT_free)
                    bfree[b] = [tok]
                gt_tok[g] = tok

            emit_gt(0)
            for g in range(GT_N):
                if g + 1 < GT_N:
                    emit_gt(g + 1)
                for eh in range(2):
                    bi = (g * 2 + eh) % 2
                    b = 2 + bi
                    for c in range(DC):
                        mm3(b, GTp[0][:, c * CAP + g * P: c * CAP + (g + 1) * P],
                            GTp[1][:, c * CAP + g * P: c * CAP + (g + 1) * P],
                            wtp[par][0][:, c * D + eh * 512: c * D + (eh + 1) * 512],
                            wtp[par][1][:, c * D + eh * 512: c * D + (eh + 1) * 512],
                            start=(c == 0), stop=(c == DC - 1),
                            waits=([gt_tok[g], w_ready[e]] + bfree[b]) if c == 0 else (),
                            inc=s_t if c == DC - 1 else None)
                    pr.emit("vector", lambda v, eh=eh, par=par, bi=bi, b=b: v.tensor_tensor(
                        t0b[bi], pb[b][:, :], bbc[par][:, eh * 512:(eh + 1) * 512],
                        ALU.add),
                        waits=[(s_t, s_t.n), bb_ready] + t0_free[bi], inc=s_v,
                        drain=False)
                    bfree[b] = [(s_v, s_v.n)]
                    pr.emit("scalar", lambda s, g=g, eh=eh, par=par, bi=bi: s.activation(
                        tE[g % 2][:, eh * 512:(eh + 1) * 512], t0b[bi], AF.Relu,
                        scale=wc[par][:, g:g + 1]),
                        waits=[(s_v, s_v.n)] + tE_free[g % 2] + t0_free[bi], inc=s_s,
                        drain=False)
                    t0_free[bi] = [(s_s, s_s.n)]
                tile_done = (s_s, s_s.n)
                pr.burst(q_st)
                pr.dma("gpsimd", hE_h[e * CAP + g * P: e * CAP + (g + 1) * P, :],
                       tE[g % 2], waits=[tile_done], inc=q_st)
                tE_free[g % 2] = [(q_st, q_st.n)]
            wt_free[par] = [(s_t, s_t.n)]
            eb_free[par] = [(s_t, s_t.n)]
            bbc_free[par] = [(s_v, s_v.n)]
            G_free[par] = [(s_t, s_t.n)]
            wc_free[par] = [(s_s, s_s.n)]
            idx_free[par] = [(q_ix, q_ix.n)]
            GT_free = [(s_t, s_t.n)]
        p3b_end = snap()
        hE_done = (q_st, q_st.n)
        adr_all = (q_st, q_st.n)

        region()
        xcb = [view(alloc(D), D) for _ in range(2)]
        hga = [view(alloc(D), D) for _ in range(2)]
        hgb = [view(alloc(D), D) for _ in range(2)]
        yt = [view(alloc(D), D) for _ in range(2)]
        xcb_free = [p3b_end, p3b_end]
        hg_free = [p3b_end, p3b_end]
        yt_free = [[], []]
        for t in range(NT):
            par = t % 2
            pr.burst(q_ld)
            pr.dma("sync", xcb[par], x2_h[t * P:(t + 1) * P, :],
                   waits=xcb_free[par], inc=q_ld)
            pr.dma("sync", IX[:, 24 + par * 2: 26 + par * 2],
                   adr_h[t * P:(t + 1) * P, :],
                   waits=xcb_free[par] + [adr_all], inc=q_ld)
            c_ready = (q_ld, q_ld.n)
            pr.burst(q_ix)
            for (dst, col) in ((hga, 24 + par * 2), (hgb, 25 + par * 2)):
                pr.emit("gpsimd", lambda gp, dst=dst, col=col, par=par: gp.indirect_dma_start(
                    dst[par], None, hE_h,
                    bass.IndirectOffsetOnAxis(ap=IX[:, col:col + 1], axis=0)),
                    waits=[c_ready, hE_done, (q_ix, q_ix.mark)] + hg_free[par],
                    inc=q_ix, amt=16)
            g_ready2 = (q_ix, q_ix.n)
            pr.emit("vector", lambda v, par=par: v.tensor_tensor(
                yt[par], xcb[par], hga[par], ALU.add),
                waits=[g_ready2] + yt_free[par], inc=s_v)
            pr.emit("vector", lambda v, par=par: v.tensor_tensor(
                yt[par], yt[par], hgb[par], ALU.add), inc=s_v)
            y_ready = (s_v, s_v.n)
            xcb_free[par] = [(s_v, s_v.n)]
            hg_free[par] = [(s_v, s_v.n)]
            pr.burst(q_st)
            pr.dma("gpsimd", y_h[t * P:(t + 1) * P, :], yt[par],
                   waits=[y_ready], inc=q_st)
            yt_free[par] = [(q_st, q_st.n)]

    for sem in (q_st, q_ix, q_sc):
        if sem.n > 0:
            pr.emit("gpsimd", lambda g, sem=sem: g.wait_ge(sem.h, sem.n))
            pr.emit("sync", lambda s, sem=sem: s.wait_ge(sem.h, sem.n))

    with nc.Block() as block:
        @block.sync
        def _(sync):
            pr.replay("sync", sync)

        @block.tensor
        def _(tensor):
            pr.replay("tensor", tensor)

        @block.vector
        def _(vector):
            pr.replay("vector", vector)

        @block.scalar
        def _(scalar):
            pr.replay("scalar", scalar)

        @block.gpsimd
        def _(gpsimd):
            pr.replay("gpsimd", gpsimd)

    return nc


def make_in_map(xb, wq, wk, wv, gw, ew, eb):
    return {
        "x": np.ascontiguousarray(xb),
        "wq": np.ascontiguousarray(wq),
        "wk": np.ascontiguousarray(wk),
        "wv": np.ascontiguousarray(wv),
        "gw": np.ascontiguousarray(gw),
        "ew": np.ascontiguousarray(ew),
        "eb": np.ascontiguousarray(eb),
        "ident": np.eye(P, dtype=np.float32),
        "onesr": np.ones((1, P), dtype=np.float32),
        "ltri": np.triu(np.ones((P, P), dtype=np.float32), 1),
        "onesc": np.ones((P, 1), dtype=np.float32),
        "iotaE": np.broadcast_to(np.arange(E, dtype=np.float32), (P, E)).copy(),
        "iotaEB": np.broadcast_to(np.arange(E, dtype=np.float32) + BIGC, (P, E)).copy(),
        "iota128": np.arange(P, dtype=np.float32).reshape(P, 1),
    }


def kernel(L0_attn_x, L0_attn_W_query, L0_attn_W_key, L0_attn_W_value,
           L0_ffn_gate_w, L0_ffn_expert_w, L0_ffn_expert_b):
    x = np.ascontiguousarray(np.asarray(L0_attn_x, dtype=np.float32))
    wq = np.asarray(L0_attn_W_query, dtype=np.float32)
    wk = np.asarray(L0_attn_W_key, dtype=np.float32)
    wv = np.asarray(L0_attn_W_value, dtype=np.float32)
    gw = np.asarray(L0_ffn_gate_w, dtype=np.float32)
    ew = np.asarray(L0_ffn_expert_w, dtype=np.float32)
    eb = np.asarray(L0_ffn_expert_b, dtype=np.float32)

    nc = build()
    in_maps = [make_in_map(x[b], wq, wk, wv, gw, ew, eb) for b in range(8)]
    res = run_bass_kernel_spmd(nc, in_maps, core_ids=list(range(8)))
    y = np.stack([np.asarray(res.results[b]["y"]) for b in range(8)], axis=0)
    return y.astype(np.float32)


# version 22
# speedup vs baseline: 1.1233x; 1.1233x over previous
import numpy as np
import concourse.bass as bass
import concourse.mybir as mybir
from concourse.bass_utils import run_bass_kernel_spmd

F32 = mybir.dt.float32
BF16 = mybir.dt.bfloat16
I32 = mybir.dt.int32
AF = mybir.ActivationFunctionType
ALU = mybir.AluOpType
AX = mybir.AxisListType

P = 128
D = 1024
S = 4096
E = 16
DC = D // P
NT = S // P
CAP = 768
GT_N = CAP // P
BIGC = 65536.0
ARENA_W = 52736

ENGINES = ("sync", "tensor", "vector", "scalar", "gpsimd")


class Prog:

    def __init__(self, nc):
        self.nc = nc
        self.ops = {e: [] for e in ENGINES}
        self._cms = []

    def sem(self, name):
        cm = self.nc.semaphore(name)
        h = cm.__enter__()
        self._cms.append(cm)

        class Sem:
            def __init__(s):
                s.h = h
                s.n = 0
                s.mark = 0
        return Sem()

    def emit(self, eng, fn, waits=(), inc=None, amt=1, drain=True):
        ws = tuple((s.h, v) for (s, v) in waits if v > 0)
        self.ops[eng].append((ws, fn, inc.h if inc is not None else None, amt, drain))
        if inc is not None:
            inc.n += amt
            return (inc, inc.n)
        return None

    def dma(self, eng, out_ap, in_ap, waits=(), inc=None, slow_ok=False):
        def fn(e, o=out_ap, i=in_ap, s=slow_ok):
            return e.dma_start(out=o, in_=i, allow_slow_non_contiguous=s)
        ws = list(waits)
        if inc is not None:
            ws.append((inc, inc.mark))
        return self.emit(eng, fn, waits=ws, inc=inc, amt=16)

    @staticmethod
    def burst(sem):
        sem.mark = sem.n

    def replay(self, eng_name, eng):
        drains = eng_name in ("vector", "scalar")
        first = True
        waited = {}
        for ws, fn, inc_h, amt, drain in self.ops[eng_name]:
            if drains and drain and not first:
                eng.drain()
            first = False
            for h, v in ws:
                if waited.get(h, -1) < v:
                    eng.wait_ge(h, v)
                    waited[h] = v
            instr = fn(eng)
            if inc_h is not None:
                instr.then_inc(inc_h, amt)


def build(phases=("p1", "pw", "p2", "p3"), debug_outputs=False):
    nc = bass.Bass("TRN2", target_bir_lowering=False)

    x_h = nc.dram_tensor("x", [S, D], F32, kind="ExternalInput").ap()
    wq_h = nc.dram_tensor("wq", [D, D], F32, kind="ExternalInput").ap()
    wk_h = nc.dram_tensor("wk", [D, D], F32, kind="ExternalInput").ap()
    wv_h = nc.dram_tensor("wv", [D, D], F32, kind="ExternalInput").ap()
    gw_h = nc.dram_tensor("gw", [E, D], F32, kind="ExternalInput").ap()
    ew_h = nc.dram_tensor("ew", [E, D, D], F32, kind="ExternalInput").ap()
    eb_h = nc.dram_tensor("eb", [E, D], F32, kind="ExternalInput").ap()
    id_h = nc.dram_tensor("ident", [P, P], F32, kind="ExternalInput").ap()
    on_h = nc.dram_tensor("onesr", [1, P], F32, kind="ExternalInput").ap()
    lt_h = nc.dram_tensor("ltri", [P, P], F32, kind="ExternalInput").ap()
    oc_h = nc.dram_tensor("onesc", [P, 1], F32, kind="ExternalInput").ap()
    ie_h = nc.dram_tensor("iotaE", [P, E], F32, kind="ExternalInput").ap()
    ib_h = nc.dram_tensor("iotaEB", [P, E], F32, kind="ExternalInput").ap()
    i128_h = nc.dram_tensor("iota128", [P, 1], F32, kind="ExternalInput").ap()

    y_h = nc.dram_tensor("y", [S, D], F32, kind="ExternalOutput").ap()

    sk = "ExternalOutput" if debug_outputs else "Internal"
    qTh_h = nc.dram_tensor("qTh", [D, S], BF16, kind=sk).ap()
    qTl_h = nc.dram_tensor("qTl", [D, S], BF16, kind=sk).ap()
    kTh_h = nc.dram_tensor("kTh", [D, S], BF16, kind=sk).ap()
    kTl_h = nc.dram_tensor("kTl", [D, S], BF16, kind=sk).ap()
    vh_h = nc.dram_tensor("vh", [S, D], BF16, kind=sk).ap()
    vl_h = nc.dram_tensor("vl", [S, D], BF16, kind=sk).ap()
    wTh_h = nc.dram_tensor("wTh", [E, D, D], BF16, kind="Internal").ap()
    wTl_h = nc.dram_tensor("wTl", [E, D, D], BF16, kind="Internal").ap()
    x2_h = nc.dram_tensor("x2", [S, D], F32, kind=sk).ap()
    tlnT_h = nc.dram_tensor("tlnT", [D, S], F32, kind=sk).ap()
    tln_h = nc.dram_tensor("tln", [S, D], F32, kind="Internal").ap()
    toks_h = nc.dram_tensor("toks", [E * CAP], I32, kind="Internal").ap()
    ws_h = nc.dram_tensor("ws", [E * CAP], F32, kind="Internal").ap()
    hE_h = nc.dram_tensor("hE", [E * CAP, D], F32, kind="Internal").ap()
    adr_h = nc.dram_tensor("adr", [S, 2], I32, kind="Internal").ap()

    arena_cm = nc.sbuf_tensor("arena", [P, ARENA_W], F32)
    AR = arena_cm.__enter__()
    ARB = AR.bitcast(BF16)
    off = [0]

    def alloc(w):
        o = off[0]
        assert o + w <= ARENA_W, f"arena overflow at {o}+{w}"
        off[0] = o + w
        return o

    def view(o, w, p0=0, pn=P):
        return AR[p0:p0 + pn, o:o + w]

    def bpair(w):
        o = alloc(w)
        return (ARB[:, 2 * o:2 * o + w], ARB[:, 2 * o + w:2 * o + 2 * w])

    o_ident = alloc(128)
    o_ones = alloc(128)
    o_cols = alloc(128)
    o_gwT = alloc(128)
    o_ltri = alloc(128)
    o_onesc = alloc(1)
    o_iotaE = alloc(E)
    o_iotaEB = alloc(E)
    o_i128 = alloc(1)
    o_rows = alloc(64)
    off[0] = (off[0] + 127) // 128 * 128
    o_base = off[0]

    ident = view(o_ident, 128)
    ones_r = view(o_ones, 128, 0, 1)
    gwT = view(o_gwT, 128)
    ltri = view(o_ltri, 128)
    onesc = view(o_onesc, 1)
    iotaE = view(o_iotaE, E)
    iotaEB = view(o_iotaEB, E)
    i128c = view(o_i128, 1)

    def rows(i, n=1):
        return view(o_rows + i, n, 0, 1)

    def cols(i, n=1):
        return view(o_cols + i, n)

    def region():
        off[0] = o_base

    ix_cm = nc.sbuf_tensor("ixbuf", [P, 28], I32)
    IX = ix_cm.__enter__()

    pb_cms = [nc.psum_tensor(f"pb{i}", [P, 512], F32) for i in range(8)]
    pb = [cm.__enter__() for cm in pb_cms]

    pr = Prog(nc)
    q_ld = pr.sem("q_ld")
    q_st = pr.sem("q_st")
    q_ix = pr.sem("q_ix")
    q_sc = pr.sem("q_sc")
    s_t = pr.sem("s_t")
    s_v = pr.sem("s_v")
    s_s = pr.sem("s_s")
    allsems = (q_ld, q_st, q_ix, q_sc, s_t, s_v, s_s)

    def snap():
        return [(s, s.n) for s in allsems if s.n > 0]

    bfree = {i: [] for i in range(8)}

    def mm3(b, shi, slo, mhi, mlo, start, stop, waits=(), inc=None):
        pr.emit("tensor", lambda t_, b=b, shi=shi, mhi=mhi, start=start: t_.matmul(
            pb[b][:, :], shi, mhi, start=start, stop=False), waits=waits)
        pr.emit("tensor", lambda t_, b=b, shi=shi, mlo=mlo: t_.matmul(
            pb[b][:, :], shi, mlo, start=False, stop=False))
        return pr.emit("tensor", lambda t_, b=b, slo=slo, mhi=mhi, stop=stop: t_.matmul(
            pb[b][:, :], slo, mhi, start=False, stop=stop), inc=inc)

    def split_psum(b, hi_ap, lo_ap, scale, hi_waits=(), lo_waits=()):
        pr.emit("scalar", lambda s, b=b, hi_ap=hi_ap, scale=scale: s.activation(
            hi_ap, pb[b][:, :], AF.Copy, scale=scale),
            waits=list(hi_waits), inc=s_s)
        pr.emit("vector", lambda v, b=b, hi_ap=hi_ap, lo_ap=lo_ap, scale=scale: v.scalar_tensor_tensor(
            lo_ap, pb[b][:, :], scale, hi_ap, ALU.mult, ALU.subtract),
            waits=[(s_s, s_s.n)] + list(lo_waits), inc=s_v, drain=False)
        return (s_v, s_v.n)

    def split_sbuf(src_ap, hi_ap, lo_ap, waits=()):
        pr.emit("scalar", lambda s, src_ap=src_ap, hi_ap=hi_ap: s.activation(
            hi_ap, src_ap, AF.Copy), waits=list(waits), inc=s_s)
        pr.emit("vector", lambda v, src_ap=src_ap, hi_ap=hi_ap, lo_ap=lo_ap: v.tensor_tensor(
            lo_ap, src_ap, hi_ap, ALU.subtract),
            waits=[(s_s, s_s.n)], inc=s_v, drain=False)
        return (s_v, s_v.n)

    pr.dma("sync", ident, id_h, inc=q_ld)
    pr.dma("sync", ones_r, on_h, inc=q_ld)
    pr.dma("sync", ltri, lt_h, inc=q_ld)
    pr.dma("sync", onesc, oc_h, inc=q_ld)
    pr.dma("sync", iotaE, ie_h, inc=q_ld)
    pr.dma("sync", iotaEB, ib_h, inc=q_ld)
    pr.dma("sync", i128c, i128_h, inc=q_ld)
    region()
    wqp = bpair(DC * D)
    wkp = bpair(DC * D)
    wvp = bpair(DC * D)
    wstg = [view(alloc(D), D) for _ in range(2)]
    xbuf = view(alloc(4 * D), 4 * D)
    xln = view(alloc(4 * D), 4 * D)
    xlnTp = bpair(DC * 512)
    qTbp = bpair(4 * D)
    kTbp = bpair(4 * D)
    o_vbufp = off[0]
    vbufp = bpair(4 * D)
    gwsrc = view(o_vbufp, D, 0, E)

    wstg_free = [[], []]
    for mi, (src_h, pair) in enumerate(((wq_h, wqp), (wk_h, wkp), (wv_h, wvp))):
        for c in range(DC):
            par = (mi * DC + c) % 2
            pr.burst(q_ld)
            pr.dma("sync", wstg[par], src_h[c * P:(c + 1) * P, :],
                   waits=wstg_free[par], inc=q_ld)
            tok = split_sbuf(wstg[par], pair[0][:, c * D:(c + 1) * D],
                             pair[1][:, c * D:(c + 1) * D],
                             waits=[(q_ld, q_ld.n)])
            wstg_free[par] = [tok]
    w_split_done = (s_v, s_v.n)

    pr.burst(q_ld)
    pr.dma("sync", gwsrc, gw_h, inc=q_ld)
    ld0 = (q_ld, q_ld.n)

    for c in range(DC):
        pr.emit("tensor",
                lambda t, c=c: t.transpose(pb[0][:, c * E:(c + 1) * E],
                                           gwsrc[:, c * P:(c + 1) * P],
                                           ident[0:E, 0:E]),
                waits=[ld0], inc=s_t if c == DC - 1 else None)
    pr.emit("vector", lambda v: v.tensor_copy(gwT, pb[0][:, 0:P]),
            waits=[(s_t, s_t.n)], inc=s_v)
    gwT_ready = (s_v, s_v.n)
    bfree[0] = [gwT_ready]

    def ln_tiles(xsrc, xdst, ntiles, wait_ready, extra_apply_waits=()):
        for t in range(ntiles):
            sl = (slice(None), slice(t * D, (t + 1) * D))
            pr.emit("vector", lambda v, t=t, sl=sl: v.reduce_sum(
                cols(0 + t), xsrc[sl], axis=AX.X), waits=[wait_ready])
            pr.emit("vector", lambda v, t=t: v.tensor_scalar(
                cols(0 + t), cols(0 + t), 1.0 / D, None, ALU.mult), inc=s_v)
        meang = (s_v, s_v.n)
        for t in range(ntiles):
            sl = (slice(None), slice(t * D, (t + 1) * D))
            pr.emit("vector", lambda v, t=t, sl=sl: v.scalar_tensor_tensor(
                xdst[sl], xsrc[sl], cols(0 + t), xsrc[sl], ALU.subtract, ALU.mult,
                accum_out=cols(4 + t)), waits=[meang], inc=s_v)
        for t in range(ntiles):
            pr.emit("vector", lambda v, t=t: v.tensor_scalar(
                cols(4 + t), cols(4 + t), 1.0 / D, 1e-5, ALU.mult, ALU.add),
                inc=s_v)
        varg = (s_v, s_v.n)
        for t in range(ntiles):
            pr.emit("scalar", lambda s, t=t: s.activation(
                cols(8 + t), cols(4 + t), AF.Sqrt),
                waits=[varg], inc=s_s)
        sqg = (s_s, s_s.n)
        for t in range(ntiles):
            pr.emit("vector", lambda v, t=t: v.reciprocal(
                cols(12 + t), cols(8 + t)), waits=[sqg], inc=s_v)
        vg = (s_v, s_v.n)
        for t in range(ntiles):
            sl = (slice(None), slice(t * D, (t + 1) * D))
            pr.emit("vector", lambda v, t=t, sl=sl: v.tensor_scalar(
                xdst[sl], xsrc[sl], cols(0 + t), cols(12 + t),
                ALU.subtract, ALU.mult),
                waits=[vg] + list(extra_apply_waits), inc=s_v)

    if "p1" in phases:
        xbuf_free = []
        xlnT_wfree = []
        qTb_free, kTb_free, vbuf_free = [], [], []
        pr.burst(q_ld)
        pr.dma("sync", xbuf.rearrange("p (t d) -> p t d", t=4),
               x_h[0:512, :].rearrange("(t p) d -> p t d", p=P), inc=q_ld)
        xld_tok = (q_ld, q_ld.n)
        for nb in range(8):
            ln_tiles(xbuf, xln, 4, xld_tok)
            xln_ready = (s_v, s_v.n)
            xbuf_free = [(s_v, s_v.n)]
            for c in range(DC):
                b = c % 2
                for t in range(4):
                    pr.emit("tensor", lambda t_, c=c, t2=t, b=b: t_.transpose(
                        pb[b][:, t2 * P:(t2 + 1) * P],
                        xln[:, t2 * D + c * P: t2 * D + (c + 1) * P], ident),
                        waits=[xln_ready] + bfree[b],
                        inc=s_t if t == 3 else None)
                tok = split_psum(b, xlnTp[0][:, c * 512:(c + 1) * 512],
                                 xlnTp[1][:, c * 512:(c + 1) * 512], 1.0,
                                 hi_waits=[(s_t, s_t.n)] + xlnT_wfree,
                                 lo_waits=xlnT_wfree)
                bfree[b] = [tok]
            xlnT_ready = tok
            for proj in ("q", "k"):
                wpair = wqp if proj == "q" else wkp
                blkp = qTbp if proj == "q" else kTbp
                scale = (1.0 / 32.0) if proj == "q" else 1.0
                free_tok = qTb_free if proj == "q" else kTb_free
                for ec in range(DC):
                    b = 2 + ec % 2
                    for c in range(DC):
                        mm3(b, wpair[0][:, c * D + ec * P: c * D + (ec + 1) * P],
                            wpair[1][:, c * D + ec * P: c * D + (ec + 1) * P],
                            xlnTp[0][:, c * 512:(c + 1) * 512],
                            xlnTp[1][:, c * 512:(c + 1) * 512],
                            start=(c == 0), stop=(c == DC - 1),
                            waits=([xlnT_ready, w_split_done] + bfree[b]) if c == 0 else (),
                            inc=s_t if c == DC - 1 else None)
                    tok = split_psum(b, blkp[0][:, ec * 512:(ec + 1) * 512],
                                     blkp[1][:, ec * 512:(ec + 1) * 512], scale,
                                     hi_waits=[(s_t, s_t.n)] + free_tok,
                                     lo_waits=free_tok)
                    bfree[b] = [tok]
                sc_done = tok
                hdst = qTh_h if proj == "q" else kTh_h
                ldst = qTl_h if proj == "q" else kTl_h
                pr.burst(q_st)
                pr.dma("gpsimd", hdst.rearrange("(c p) n -> p c n", p=P)[:, :, nb * 512:(nb + 1) * 512],
                       blkp[0].rearrange("p (c n) -> p c n", c=DC), waits=[sc_done], inc=q_st)
                pr.dma("gpsimd", ldst.rearrange("(c p) n -> p c n", p=P)[:, :, nb * 512:(nb + 1) * 512],
                       blkp[1].rearrange("p (c n) -> p c n", c=DC), waits=[sc_done], inc=q_st)
                if proj == "q":
                    qTb_free = [(q_st, q_st.n)]
                else:
                    kTb_free = [(q_st, q_st.n)]
            pr.burst(q_st)
            for t in range(4):
                for eh in range(2):
                    b = 4 + (t * 2 + eh) % 2
                    for c in range(DC):
                        mm3(b, xlnTp[0][:, c * 512 + t * P: c * 512 + (t + 1) * P],
                            xlnTp[1][:, c * 512 + t * P: c * 512 + (t + 1) * P],
                            wvp[0][:, c * D + eh * 512: c * D + (eh + 1) * 512],
                            wvp[1][:, c * D + eh * 512: c * D + (eh + 1) * 512],
                            start=(c == 0), stop=(c == DC - 1),
                            waits=([xlnT_ready, w_split_done] + bfree[b]) if c == 0 else (),
                            inc=s_t if c == DC - 1 else None)
                    tok = split_psum(b, vbufp[0][:, t * D + eh * 512: t * D + (eh + 1) * 512],
                                     vbufp[1][:, t * D + eh * 512: t * D + (eh + 1) * 512], 1.0,
                                     hi_waits=[(s_t, s_t.n)] + vbuf_free,
                                     lo_waits=vbuf_free)
                    bfree[b] = [tok]
                pr.dma("gpsimd", vh_h[(nb * 4 + t) * P:(nb * 4 + t + 1) * P, :],
                       vbufp[0][:, t * D:(t + 1) * D], waits=[tok], inc=q_st)
                pr.dma("gpsimd", vl_h[(nb * 4 + t) * P:(nb * 4 + t + 1) * P, :],
                       vbufp[1][:, t * D:(t + 1) * D], waits=[tok], inc=q_st)
            vbuf_free = [(q_st, q_st.n)]
            xlnT_wfree = [(s_t, s_t.n)]
            if nb < 7:
                pr.burst(q_ld)
                pr.dma("sync", xbuf.rearrange("p (t d) -> p t d", t=4),
                       x_h[(nb + 1) * 512:(nb + 2) * 512, :].rearrange(
                           "(t p) d -> p t d", p=P),
                       waits=xbuf_free, inc=q_ld)
                xld_tok = (q_ld, q_ld.n)

    p1_end = snap()
    pw_end = p1_end

    if "p2" in phases:
        region()
        kTgp = [bpair(2048) for _ in range(2)]
        qTb2p = [bpair(4096) for _ in range(2)]
        vb2p = [bpair(4 * D) for _ in range(2)]
        PTbp = [bpair(4096) for _ in range(2)]
        Sb = [view(alloc(4 * D), 4 * D) for _ in range(2)]
        Pb = [view(alloc(4 * D), 4 * D) for _ in range(2)]
        pmax = [view(alloc(16), 16) for _ in range(2)]
        xres = view(alloc(D), D)
        x2t = view(alloc(D), D)
        x2ln = view(alloc(D), D)
        tlnTb = view(alloc(D), D)
        ewstg = [view(alloc(512), 512) for _ in range(2)]
        ewpp = [bpair(512) for _ in range(2)]
        ewstg_free = [pw_end, pw_end]
        ewpp_free = [pw_end, pw_end]
        ew_ld_tok = [None, None]

        def ew_split1(e, k):
            par = k % 2
            c = k // 2
            hf = k % 2
            tok = split_sbuf(ewstg[par], ewpp[par][0], ewpp[par][1],
                             waits=[ew_ld_tok[par]] + ewpp_free[par])
            ewstg_free[par] = [tok]
            pr.burst(q_st)
            pr.dma("gpsimd", wTh_h[e][c * P:(c + 1) * P, hf * 512:(hf + 1) * 512],
                   ewpp[par][0], waits=[tok], inc=q_st)
            pr.dma("gpsimd", wTl_h[e][c * P:(c + 1) * P, hf * 512:(hf + 1) * 512],
                   ewpp[par][1], waits=[tok], inc=q_st)
            ewpp_free[par] = [(q_st, q_st.n)]

        def ew_pt(e, k):
            par = k % 2
            c = k // 2
            hf = k % 2
            pr.burst(q_sc)
            pr.dma("gpsimd", ewstg[par],
                   ew_h[e][c * P:(c + 1) * P, hf * 512:(hf + 1) * 512],
                   waits=ewstg_free[par], inc=q_sc)
            ew_ld_tok[par] = (q_sc, q_sc.n)
            if k >= 1:
                ew_split1(e, k - 1)

        kTg_free = [pw_end, pw_end]
        qTb2_free = [pw_end, pw_end]
        vb2_free = [pw_end, pw_end]
        P_free = [pw_end, pw_end]
        PT_free = [pw_end, pw_end]
        x2t_free = []
        x2ln_free = []
        tlnTb_free = []
        for i in range(8):
            bfree[i] = bfree[i] + pw_end

        def ln_one(xsrc, xdst, wait_ready, extra_apply_waits=()):
            pr.emit("vector", lambda v: v.reduce_sum(cols(0), xsrc, axis=AX.X),
                    waits=[wait_ready])
            pr.emit("vector", lambda v: v.tensor_scalar(
                cols(0), cols(0), 1.0 / D, None, ALU.mult), inc=s_v)
            g1 = (s_v, s_v.n)
            pr.emit("vector", lambda v: v.scalar_tensor_tensor(
                xdst, xsrc, cols(0), xsrc, ALU.subtract, ALU.mult,
                accum_out=cols(1)), waits=[g1] + list(extra_apply_waits), inc=s_v)
            pr.emit("vector", lambda v: v.tensor_scalar(
                cols(1), cols(1), 1.0 / D, 1e-5, ALU.mult, ALU.add), inc=s_v)
            pr.emit("scalar", lambda s: s.activation(
                cols(2), cols(1), AF.Sqrt),
                waits=[(s_v, s_v.n)], inc=s_s)
            pr.emit("vector", lambda v: v.reciprocal(cols(3), cols(2)),
                    waits=[(s_s, s_s.n)], inc=s_v)
            g2 = (s_v, s_v.n)
            pr.emit("vector", lambda v: v.tensor_scalar(
                xdst, xsrc, cols(0), cols(3), ALU.subtract, ALU.mult),
                waits=[g2] + list(extra_apply_waits), inc=s_v)

        for jg in range(16):
            kpar = jg % 2
            pr.burst(q_ld)
            pr.dma("sync", kTgp[kpar][0].rearrange("p (c n) -> p c n", c=DC),
                   kTh_h.rearrange("(c p) n -> p c n", p=P)[:, :, jg * 256:(jg + 1) * 256],
                   waits=kTg_free[kpar], inc=q_ld)
            pr.dma("sync", kTgp[kpar][1].rearrange("p (c n) -> p c n", c=DC),
                   kTl_h.rearrange("(c p) n -> p c n", p=P)[:, :, jg * 256:(jg + 1) * 256],
                   waits=kTg_free[kpar], inc=q_ld)
            kTg_ready = (q_ld, q_ld.n)
            for ib in range(8):
                par = ib % 2
                pr.burst(q_ld)
                pr.dma("sync", qTb2p[par][0].rearrange("p (c n) -> p c n", c=DC),
                       qTh_h.rearrange("(c p) n -> p c n", p=P)[:, :, ib * 512:(ib + 1) * 512],
                       waits=qTb2_free[par], inc=q_ld)
                pr.dma("sync", qTb2p[par][1].rearrange("p (c n) -> p c n", c=DC),
                       qTl_h.rearrange("(c p) n -> p c n", p=P)[:, :, ib * 512:(ib + 1) * 512],
                       waits=qTb2_free[par], inc=q_ld)
                q_ready = (q_ld, q_ld.n)
                for jt in range(2):
                    b = (ib * 2 + jt) % 2
                    for ec in range(DC):
                        mm3(b, kTgp[kpar][0][:, ec * 256 + jt * P: ec * 256 + (jt + 1) * P],
                            kTgp[kpar][1][:, ec * 256 + jt * P: ec * 256 + (jt + 1) * P],
                            qTb2p[par][0][:, ec * 512:(ec + 1) * 512],
                            qTb2p[par][1][:, ec * 512:(ec + 1) * 512],
                            start=(ec == 0), stop=(ec == DC - 1),
                            waits=([kTg_ready, q_ready] + bfree[b]) if ec == 0 else (),
                            inc=s_t if ec == DC - 1 else None)
                    pr.emit("scalar", lambda s, ib=ib, jt=jt, b=b: s.activation(
                        Sb[jt][:, ib * 512:(ib + 1) * 512], pb[b][:, :], AF.Copy, scale=1.0),
                        waits=[(s_t, s_t.n)], inc=s_s)
                    bfree[b] = [(s_s, s_s.n)]
                    pr.emit("vector", lambda v, ib=ib, jt=jt: v.reduce_max(
                        pmax[jt][:, ib:ib + 1], Sb[jt][:, ib * 512:(ib + 1) * 512],
                        axis=AX.X), waits=[(s_s, s_s.n)], inc=s_v, drain=False)
                qTb2_free[par] = [(s_t, s_t.n)]
                ew_pt(jg, ib)
            s_done = (s_s, s_s.n)
            exp_tok = [None, None]
            for jt in range(2):
                pr.emit("vector", lambda v, jt=jt: v.reduce_max(
                    cols(16 + jt), pmax[jt], axis=AX.X, negate=True),
                    waits=[(s_v, s_v.n)], inc=s_v)
                pr.emit("scalar", lambda s, jt=jt: s.activation(
                    Pb[jt], Sb[jt], AF.Exp, bias=cols(16 + jt),
                    accum_out=cols(18 + jt)),
                    waits=[(s_v, s_v.n)] + P_free[jt], inc=s_s)
                exp_tok[jt] = (s_s, s_s.n)
                pr.emit("vector", lambda v, jt=jt: v.reciprocal(
                    cols(20 + jt), cols(18 + jt)), waits=[(s_s, s_s.n)], inc=s_v)
            inv_ready = (s_v, s_v.n)
            sp_tok = [[None] * 8, [None] * 8]

            def emit_pt(g):
                for jt in range(2):
                    b = 2 + (g * 2 + jt) % 2
                    for k in range(4):
                        ic = g * 4 + k
                        pr.emit("tensor", lambda t_, ic=ic, k=k, jt=jt, b=b: t_.transpose(
                            pb[b][:, k * P:(k + 1) * P],
                            Pb[jt][:, ic * P:(ic + 1) * P], ident),
                            waits=[exp_tok[jt]] + bfree[b],
                            inc=s_t if k == 3 else None)
                    tok = split_psum(b, PTbp[jt][0][:, g * 512:(g + 1) * 512],
                                     PTbp[jt][1][:, g * 512:(g + 1) * 512], 1.0,
                                     hi_waits=[(s_t, s_t.n)] + PT_free[jt],
                                     lo_waits=PT_free[jt])
                    sp_tok[jt][g] = tok
                    bfree[b] = [tok]

            emit_pt(0)
            for vb in range(8):
                if vb + 1 < 8:
                    emit_pt(vb + 1)
                par = vb % 2
                pr.burst(q_ld)
                pr.dma("sync", vb2p[par][0].rearrange("p (cc d) -> p cc d", cc=4),
                       vh_h[vb * 512:(vb + 1) * 512, :].rearrange("(cc p) d -> p cc d", p=P),
                       waits=vb2_free[par], inc=q_ld)
                pr.dma("sync", vb2p[par][1].rearrange("p (cc d) -> p cc d", cc=4),
                       vl_h[vb * 512:(vb + 1) * 512, :].rearrange("(cc p) d -> p cc d", p=P),
                       waits=vb2_free[par], inc=q_ld)
                v_ready = (q_ld, q_ld.n)
                for cc in range(4):
                    ic = vb * 4 + cc
                    for jt in range(2):
                        for eh in range(2):
                            b = 4 + jt * 2 + eh
                            last = (cc == 3 and jt == 1 and eh == 1)
                            mm3(b, PTbp[jt][0][:, ic * P:(ic + 1) * P],
                                PTbp[jt][1][:, ic * P:(ic + 1) * P],
                                vb2p[par][0][:, cc * D + eh * 512: cc * D + (eh + 1) * 512],
                                vb2p[par][1][:, cc * D + eh * 512: cc * D + (eh + 1) * 512],
                                start=(ic == 0), stop=(ic == 31),
                                waits=[sp_tok[0][vb], sp_tok[1][vb], v_ready]
                                      + (bfree[b] if ic == 0 else []),
                                inc=s_t if last else None)
                vb2_free[par] = [(s_t, s_t.n)]
                ew_pt(jg, 8 + vb)
            ew_split1(jg, 15)
            for jt in range(2):
                P_free[jt] = [(s_t, s_t.n)]
            av_done = (s_t, s_t.n)
            PT_free = [[av_done], [av_done]]
            for jt in range(2):
                jr = jg * 2 + jt
                pr.burst(q_ld)
                pr.dma("sync", xres, x_h[jr * P:(jr + 1) * P, :],
                       waits=x2t_free, inc=q_ld)
                xr_ready = (q_ld, q_ld.n)
                for eh in range(2):
                    b = 4 + jt * 2 + eh
                    pr.emit("vector", lambda v, jt=jt, eh=eh, b=b: v.scalar_tensor_tensor(
                        x2t[:, eh * 512:(eh + 1) * 512], pb[b][:, :], cols(20 + jt),
                        xres[:, eh * 512:(eh + 1) * 512], ALU.mult, ALU.add),
                        waits=[av_done, xr_ready, inv_ready] + x2t_free, inc=s_v,
                        drain=False)
                    bfree[b] = [(s_v, s_v.n)]
                x2_ready = (s_v, s_v.n)
                pr.burst(q_st)
                pr.dma("gpsimd", x2_h[jr * P:(jr + 1) * P, :], x2t,
                       waits=[x2_ready], inc=q_st)
                ln_one(x2t, x2ln, x2_ready, extra_apply_waits=x2ln_free)
                ln2_done = (s_v, s_v.n)
                x2t_free = [(s_v, s_v.n), (q_st, q_st.n)]
                pr.burst(q_st)
                pr.dma("gpsimd", tln_h[jr * P:(jr + 1) * P, :], x2ln,
                       waits=[ln2_done], inc=q_st)
                tln_store_tok = (q_st, q_st.n)
                for half in range(2):
                    b = 2 + half
                    for k in range(4):
                        c = half * 4 + k
                        pr.emit("tensor", lambda t_, c=c, k=k, b=b: t_.transpose(
                            pb[b][:, k * P:(k + 1) * P],
                            x2ln[:, c * P:(c + 1) * P], ident),
                            waits=[ln2_done] + bfree[b],
                            inc=s_t if k == 3 else None)
                    pr.emit("vector", lambda v, half=half, b=b: v.tensor_copy(
                        tlnTb[:, half * 512:(half + 1) * 512], pb[b][:, :]),
                        waits=[(s_t, s_t.n)] + tlnTb_free, inc=s_v, drain=False)
                    bfree[b] = [(s_v, s_v.n)]
                x2ln_free = [(s_t, s_t.n), tln_store_tok]
                tb_ready = (s_v, s_v.n)
                pr.burst(q_st)
                pr.dma("gpsimd", tlnT_h.rearrange("(c p) n -> p c n", p=P)[:, :, jr * P:(jr + 1) * P],
                       tlnTb.rearrange("p (c n) -> p c n", c=DC), waits=[tb_ready], inc=q_st)
                tlnTb_free = [(q_st, q_st.n)]
            kTg_free[kpar] = [(s_t, s_t.n)]

    p2_end = snap()

    if "p3" in phases:
        region()
        tln = view(alloc(DC * D), DC * D)
        lgall = view(alloc(P), P)
        pall = view(alloc(P), P)
        cndall = view(alloc(P), P)
        geall = view(alloc(P), P)
        ohall1 = view(alloc(P), P)
        ohall2 = view(alloc(P), P)
        t8all = view(alloc(64), 64)
        sta = view(alloc(64), 64)
        pmd = view(alloc(E), E)
        zsb = view(alloc(E * CAP // P), E * CAP // P)

        p3a_start = p2_end
        slab_free = p3a_start

        pr.emit("vector", lambda v: v.memset(zsb, 0.0), waits=p3a_start, inc=s_v)
        zs_ready = (s_v, s_v.n)
        pr.burst(q_ix)
        pr.dma("gpsimd", toks_h.rearrange("(p g) -> p g", p=P), zsb,
               waits=[zs_ready], inc=q_ix)
        pr.dma("gpsimd", ws_h.rearrange("(p g) -> p g", p=P), zsb,
               waits=[zs_ready], inc=q_ix)
        zs_done = (q_ix, q_ix.n)

        pr.emit("vector", lambda v: v.memset(rows(0, 64), 0.0), inc=s_v)
        base_tok = (s_v, s_v.n)

        scat_done_prev = zs_done
        adr_done_prev = (q_st, 0)
        for slab in range(4):
            pr.burst(q_ld)
            pr.dma("sync", tln.rearrange("p (c n) -> p c n", c=DC),
                   tlnT_h.rearrange("(c p) n -> p c n", p=P)[:, :, slab * D:(slab + 1) * D],
                   waits=slab_free, inc=q_ld)
            slab_ready = (q_ld, q_ld.n)
            for ntile in range(8):
                for c in range(DC):
                    pr.emit("tensor", lambda t_, c=c, ntile=ntile: t_.matmul(
                        pb[0][:, ntile * E:(ntile + 1) * E],
                        tln[:, c * D + ntile * P: c * D + (ntile + 1) * P],
                        gwT[:, c * E:(c + 1) * E],
                        start=(c == 0), stop=(c == DC - 1)),
                        waits=([slab_ready, gwT_ready] + bfree[0]) if (ntile == 0 and c == 0) else (),
                        inc=s_t if (ntile == 7 and c == DC - 1) else None)
            pr.emit("scalar", lambda s: s.activation(
                lgall, pb[0][:, 0:P], AF.Copy, scale=1.0),
                waits=[(s_t, s_t.n)], inc=s_s)
            lg_tok = (s_s, s_s.n)
            bfree[0] = [lg_tok]
            for t in range(8):
                pr.emit("vector", lambda v, t=t: v.reduce_max(
                    sta[:, 0 + t:1 + t], lgall[:, t * E:(t + 1) * E], axis=AX.X,
                    negate=True),
                    waits=[lg_tok, scat_done_prev, adr_done_prev], inc=s_v)
            gmx_tok = (s_v, s_v.n)
            for t in range(8):
                pr.emit("scalar", lambda s, t=t: s.activation(
                    pall[:, t * E:(t + 1) * E], lgall[:, t * E:(t + 1) * E], AF.Exp,
                    bias=sta[:, 0 + t:1 + t], accum_out=sta[:, 8 + t:9 + t]),
                    waits=[gmx_tok], inc=s_s)
            exp_tok2 = (s_s, s_s.n)
            for t in range(8):
                pr.emit("vector", lambda v, t=t: v.reciprocal(
                    sta[:, 16 + t:17 + t], sta[:, 8 + t:9 + t]),
                    waits=[exp_tok2], inc=s_v)
            rec_tok = (s_v, s_v.n)
            for t in range(8):
                pr.emit("vector", lambda v, t=t: v.tensor_scalar(
                    pall[:, t * E:(t + 1) * E], pall[:, t * E:(t + 1) * E],
                    sta[:, 16 + t:17 + t], None, ALU.mult),
                    waits=[rec_tok], inc=s_v)
            for t in range(8):
                pr.emit("vector", lambda v, t=t: v.max(
                    t8all[:, t * 8:(t + 1) * 8], pall[:, t * E:(t + 1) * E]),
                    inc=s_v)
            t8_tok = (s_v, s_v.n)
            for t in range(8):
                pr.emit("vector", lambda v, t=t: v.tensor_scalar(
                    geall[:, t * E:(t + 1) * E], pall[:, t * E:(t + 1) * E],
                    t8all[:, t * 8:t * 8 + 1], None, ALU.is_ge),
                    waits=[t8_tok], inc=s_v)
            for t in range(8):
                pr.emit("vector", lambda v, t=t: v.scalar_tensor_tensor(
                    cndall[:, t * E:(t + 1) * E], geall[:, t * E:(t + 1) * E],
                    -BIGC, iotaEB, ALU.mult, ALU.add), inc=s_v)
            for t in range(8):
                pr.emit("vector", lambda v, t=t: v.tensor_reduce(
                    sta[:, 24 + t:25 + t], cndall[:, t * E:(t + 1) * E],
                    axis=AX.X, op=ALU.min), inc=s_v)
            e1_tok = (s_v, s_v.n)
            for t in range(8):
                pr.emit("vector", lambda v, t=t: v.tensor_scalar(
                    ohall1[:, t * E:(t + 1) * E], iotaE, sta[:, 24 + t:25 + t],
                    None, ALU.is_equal), waits=[e1_tok], inc=s_v)
            for t in range(8):
                pr.emit("vector", lambda v, t=t: v.tensor_scalar(
                    geall[:, t * E:(t + 1) * E], pall[:, t * E:(t + 1) * E],
                    t8all[:, t * 8 + 1:t * 8 + 2], None, ALU.is_ge), inc=s_v)
            for t in range(8):
                pr.emit("vector", lambda v, t=t: v.scalar_tensor_tensor(
                    cndall[:, t * E:(t + 1) * E], geall[:, t * E:(t + 1) * E],
                    -BIGC, iotaEB, ALU.mult, ALU.add), inc=s_v)
            oh1_tok = (s_v, s_v.n)
            for t in range(8):
                pr.emit("vector", lambda v, t=t: v.scalar_tensor_tensor(
                    cndall[:, t * E:(t + 1) * E], ohall1[:, t * E:(t + 1) * E],
                    BIGC, cndall[:, t * E:(t + 1) * E], ALU.mult, ALU.add),
                    waits=[oh1_tok], inc=s_v)
            for t in range(8):
                pr.emit("vector", lambda v, t=t: v.tensor_reduce(
                    sta[:, 32 + t:33 + t], cndall[:, t * E:(t + 1) * E],
                    axis=AX.X, op=ALU.min), inc=s_v)
            e2_tok = (s_v, s_v.n)
            for t in range(8):
                pr.emit("vector", lambda v, t=t: v.tensor_scalar(
                    ohall2[:, t * E:(t + 1) * E], iotaE, sta[:, 32 + t:33 + t],
                    None, ALU.is_equal), waits=[e2_tok], inc=s_v)
            oh_all_tok = (s_v, s_v.n)
            for ntile in range(8):
                gt = slab * 8 + ntile
                oh1 = ohall1[:, ntile * E:(ntile + 1) * E]
                oh2 = ohall2[:, ntile * E:(ntile + 1) * E]
                pr.emit("tensor", lambda t_, oh1=oh1: t_.matmul(
                    pb[1][:, 0:E], ltri, oh1, start=True, stop=False),
                    waits=[oh_all_tok, base_tok] + bfree[1] + bfree[2] + bfree[3])
                pr.emit("tensor", lambda t_: t_.matmul(
                    pb[1][:, 0:E], ones_r, rows(0, E), start=False, stop=True))
                pr.emit("tensor", lambda t_, oh1=oh1: t_.matmul(
                    pb[2][0:1, 0:E], onesc, oh1, start=True, stop=True))
                pr.emit("tensor", lambda t_, oh2=oh2: t_.matmul(
                    pb[3][:, 0:E], ltri, oh2, start=True, stop=False),
                    inc=s_t)
                mm1 = (s_t, s_t.n)
                pr.emit("vector", lambda v, oh1=oh1: v.scalar_tensor_tensor(
                    pmd, pb[1][:, 0:E], 1.0, oh1, ALU.mult, ALU.mult,
                    accum_out=cols(29)), waits=[mm1], inc=s_v)
                pr.emit("vector", lambda v: v.tensor_tensor(
                    rows(16, E), rows(0, E), pb[2][0:1, 0:E], ALU.add), inc=s_v)
                base_b_tok = (s_v, s_v.n)
                pr.emit("tensor", lambda t_: t_.matmul(
                    pb[3][:, 0:E], ones_r, rows(16, E), start=False, stop=True),
                    waits=[base_b_tok])
                pr.emit("tensor", lambda t_, oh2=oh2: t_.matmul(
                    pb[2][0:1, 16:32], onesc, oh2, start=True, stop=True),
                    inc=s_t)
                mm2 = (s_t, s_t.n)
                pr.emit("vector", lambda v, oh2=oh2: v.scalar_tensor_tensor(
                    pmd, pb[3][:, 0:E], 1.0, oh2, ALU.mult, ALU.mult,
                    accum_out=cols(30)), waits=[mm2], inc=s_v)
                pr.emit("vector", lambda v: v.tensor_tensor(
                    rows(0, E), rows(16, E), pb[2][0:1, 16:32], ALU.add), inc=s_v)
                base_tok = (s_v, s_v.n)
                bfree[1] = [base_tok]
                bfree[2] = [base_tok]
                bfree[3] = [base_tok]
                g4 = (s_v, s_v.n)
                pr.emit("vector", lambda v, ntile=ntile: v.tensor_scalar(
                    cols(72), sta[:, 24 + ntile:25 + ntile], float(CAP), cols(29),
                    ALU.mult, ALU.add), waits=[g4], inc=s_v)
                pr.emit("vector", lambda v, ntile=ntile: v.tensor_scalar(
                    cols(74), sta[:, 24 + ntile:25 + ntile], float(CAP),
                    float(CAP - 1), ALU.mult, ALU.add), inc=s_v)
                pr.emit("vector", lambda v: v.tensor_tensor(
                    cols(72), cols(72), cols(74), ALU.min), inc=s_v)
                pr.emit("vector", lambda v, ntile=ntile: v.tensor_scalar(
                    cols(73), sta[:, 32 + ntile:33 + ntile], float(CAP), cols(30),
                    ALU.mult, ALU.add), inc=s_v)
                pr.emit("vector", lambda v, ntile=ntile: v.tensor_scalar(
                    cols(74), sta[:, 32 + ntile:33 + ntile], float(CAP),
                    float(CAP - 1), ALU.mult, ALU.add), inc=s_v)
                pr.emit("vector", lambda v: v.tensor_tensor(
                    cols(73), cols(73), cols(74), ALU.min), inc=s_v)
                pr.emit("vector", lambda v, gt=gt: v.tensor_scalar(
                    cols(75), i128c, float(gt * P), None, ALU.add), inc=s_v)
                pr.emit("vector", lambda v: v.tensor_copy(IX[:, 24:25], cols(72)),
                        waits=[scat_done_prev, adr_done_prev], inc=s_v)
                pr.emit("vector", lambda v: v.tensor_copy(IX[:, 25:26], cols(73)), inc=s_v)
                pr.emit("vector", lambda v: v.tensor_copy(IX[:, 26:27], cols(75)), inc=s_v)
                route_ready = (s_v, s_v.n)
                pr.burst(q_st)
                pr.dma("gpsimd", adr_h[gt * P:(gt + 1) * P, :], IX[:, 24:26],
                       waits=[route_ready], inc=q_st)
                adr_done_prev = (q_st, q_st.n)
                pr.burst(q_ix)
                mark = (q_ix, q_ix.mark)
                toks2 = toks_h.rearrange("(a b) -> a b", b=1)
                ws2 = ws_h.rearrange("(a b) -> a b", b=1)
                for (ixc, vals) in ((24, IX[:, 26:27]), (25, IX[:, 26:27])):
                    pr.emit("gpsimd", lambda g, ixc=ixc, vals=vals, t2=toks2: g.indirect_dma_start(
                        t2, bass.IndirectOffsetOnAxis(ap=IX[:, ixc:ixc + 1], axis=0),
                        vals, None),
                        waits=[route_ready, mark], inc=q_ix, amt=16)
                for (ixc, tt) in ((24, 0), (25, 1)):
                    wv_ap = t8all[:, ntile * 8 + tt: ntile * 8 + tt + 1]
                    pr.emit("gpsimd", lambda g, ixc=ixc, wv_ap=wv_ap, w2=ws2: g.indirect_dma_start(
                        w2, bass.IndirectOffsetOnAxis(ap=IX[:, ixc:ixc + 1], axis=0),
                        wv_ap, None),
                        waits=[route_ready, mark], inc=q_ix, amt=16)
                scat_done_prev = (q_ix, q_ix.n)
            slab_free = [(s_t, s_t.n)]
        routing_done = (q_ix, q_ix.n)
        p3a_end = snap()

        region()
        Gb = [view(alloc(GT_N * D), GT_N * D) for _ in range(2)]
        GTp = bpair(DC * CAP)
        GT3h = GTp[0].rearrange("p (c n) -> p c n", c=DC)
        GT3l = GTp[1].rearrange("p (c n) -> p c n", c=DC)
        wtp = [bpair(DC * D) for _ in range(2)]
        wc = [view(alloc(8), 8) for _ in range(2)]
        ebr = [view(alloc(D), D, 0, 1) for _ in range(2)]
        bbc = [view(alloc(D), D) for _ in range(2)]
        t0b = [view(alloc(512), 512) for _ in range(2)]
        tE = [view(alloc(D), D) for _ in range(2)]

        wt_free = [p3a_end, p3a_end]
        eb_free = [p3a_end, p3a_end]
        wc_free = [p3a_end, p3a_end]
        idx_free = [p3a_end, p3a_end]
        bbc_free = [p3a_end, p3a_end]
        G_free = [p3a_end, p3a_end]
        GT_free = []
        t0_free = [[], []]
        tE_free = [[], []]
        w_ready = [None] * E
        g_ready = [None] * E

        def p3_loads(e):
            par = e % 2
            pr.burst(q_ld)
            for c in range(DC):
                pr.emit("sync", lambda s, e=e, c=c, par=par: s.dma_start(
                    out=wtp[par][0][:, c * D:(c + 1) * D],
                    in_=wTh_h[e][:, c * P:(c + 1) * P], transpose=True),
                    waits=wt_free[par] + [(q_ld, q_ld.mark)], inc=q_ld, amt=16)
                pr.emit("sync", lambda s, e=e, c=c, par=par: s.dma_start(
                    out=wtp[par][1][:, c * D:(c + 1) * D],
                    in_=wTl_h[e][:, c * P:(c + 1) * P], transpose=True),
                    waits=wt_free[par] + [(q_ld, q_ld.mark)], inc=q_ld, amt=16)
            pr.dma("sync", ebr[par], eb_h[e:e + 1, :], waits=eb_free[par], inc=q_ld)
            pr.dma("sync", wc[par][:, 0:GT_N],
                   ws_h[e * CAP:(e + 1) * CAP].rearrange("(g p) -> p g", p=P),
                   waits=wc_free[par] + [routing_done], inc=q_ld, slow_ok=True)
            pr.dma("sync", IX[:, par * 6 + 0: par * 6 + GT_N],
                   toks_h[e * CAP:(e + 1) * CAP].rearrange("(g p) -> p g", p=P),
                   waits=wc_free[par] + idx_free[par] + [routing_done], inc=q_ld,
                   slow_ok=True)
            w_ready[e] = (q_ld, q_ld.n)

        def p3_gathers(e):
            par = e % 2
            pr.burst(q_ix)
            for g in range(GT_N):
                pr.emit("gpsimd", lambda gp, g=g, par=par: gp.indirect_dma_start(
                    Gb[par][:, g * D:(g + 1) * D], None, tln_h,
                    bass.IndirectOffsetOnAxis(
                        ap=IX[:, par * 6 + g: par * 6 + g + 1], axis=0)),
                    waits=[w_ready[e], routing_done, (q_ix, q_ix.mark)] + G_free[par],
                    inc=q_ix, amt=16)
            g_ready[e] = (q_ix, q_ix.n)

        p3_loads(0)
        p3_gathers(0)
        for e in range(E):
            par = e % 2
            if e + 1 < E:
                p3_loads(e + 1)
                p3_gathers(e + 1)
            for eh in range(2):
                pr.emit("tensor", lambda t_, eh=eh, par=par: t_.matmul(
                    pb[4 + eh][:, :], ones_r, ebr[par][:, eh * 512:(eh + 1) * 512],
                    start=True, stop=True),
                    waits=[w_ready[e]] + bfree[4 + eh],
                    inc=s_t if eh == 1 else None)
            for eh in range(2):
                pr.emit("vector", lambda v, eh=eh, par=par: v.tensor_copy(
                    bbc[par][:, eh * 512:(eh + 1) * 512], pb[4 + eh][:, :]),
                    waits=[(s_t, s_t.n)] + bbc_free[par], inc=s_v, drain=False)
                bfree[4 + eh] = [(s_v, s_v.n)]
            bb_ready = (s_v, s_v.n)
            gt_tok = [None] * GT_N

            def emit_gt(g, par=par, e=e):
                for half in range(2):
                    b = 6 + (g * 2 + half) % 2
                    for k in range(4):
                        c = half * 4 + k
                        pr.emit("tensor", lambda t_, g=g, c=c, k=k, par=par, b=b: t_.transpose(
                            pb[b][:, k * P:(k + 1) * P],
                            Gb[par][:, g * D + c * P: g * D + (c + 1) * P], ident),
                            waits=[g_ready[e]] + bfree[b],
                            inc=s_t if k == 3 else None)
                    tok = split_psum(b, GT3h[:, half * 4:(half + 1) * 4, g * P:(g + 1) * P],
                                     GT3l[:, half * 4:(half + 1) * 4, g * P:(g + 1) * P], 1.0,
                                     hi_waits=[(s_t, s_t.n)] + GT_free,
                                     lo_waits=GT_free)
                    bfree[b] = [tok]
                gt_tok[g] = tok

            emit_gt(0)
            for g in range(GT_N):
                if g + 1 < GT_N:
                    emit_gt(g + 1)
                for eh in range(2):
                    bi = (g * 2 + eh) % 2
                    b = 2 + bi
                    for c in range(DC):
                        mm3(b, GTp[0][:, c * CAP + g * P: c * CAP + (g + 1) * P],
                            GTp[1][:, c * CAP + g * P: c * CAP + (g + 1) * P],
                            wtp[par][0][:, c * D + eh * 512: c * D + (eh + 1) * 512],
                            wtp[par][1][:, c * D + eh * 512: c * D + (eh + 1) * 512],
                            start=(c == 0), stop=(c == DC - 1),
                            waits=([gt_tok[g], w_ready[e]] + bfree[b]) if c == 0 else (),
                            inc=s_t if c == DC - 1 else None)
                    pr.emit("vector", lambda v, eh=eh, par=par, bi=bi, b=b: v.tensor_tensor(
                        t0b[bi], pb[b][:, :], bbc[par][:, eh * 512:(eh + 1) * 512],
                        ALU.add),
                        waits=[(s_t, s_t.n), bb_ready] + t0_free[bi], inc=s_v,
                        drain=False)
                    bfree[b] = [(s_v, s_v.n)]
                    pr.emit("scalar", lambda s, g=g, eh=eh, par=par, bi=bi: s.activation(
                        tE[g % 2][:, eh * 512:(eh + 1) * 512], t0b[bi], AF.Relu,
                        scale=wc[par][:, g:g + 1]),
                        waits=[(s_v, s_v.n)] + tE_free[g % 2] + t0_free[bi], inc=s_s,
                        drain=False)
                    t0_free[bi] = [(s_s, s_s.n)]
                tile_done = (s_s, s_s.n)
                pr.burst(q_st)
                pr.dma("gpsimd", hE_h[e * CAP + g * P: e * CAP + (g + 1) * P, :],
                       tE[g % 2], waits=[tile_done], inc=q_st)
                tE_free[g % 2] = [(q_st, q_st.n)]
            wt_free[par] = [(s_t, s_t.n)]
            eb_free[par] = [(s_t, s_t.n)]
            bbc_free[par] = [(s_v, s_v.n)]
            G_free[par] = [(s_t, s_t.n)]
            wc_free[par] = [(s_s, s_s.n)]
            idx_free[par] = [(q_ix, q_ix.n)]
            GT_free = [(s_t, s_t.n)]
        p3b_end = snap()
        hE_done = (q_st, q_st.n)
        adr_all = (q_st, q_st.n)

        region()
        xcb = [view(alloc(D), D) for _ in range(2)]
        hga = [view(alloc(D), D) for _ in range(2)]
        hgb = [view(alloc(D), D) for _ in range(2)]
        yt = [view(alloc(D), D) for _ in range(2)]
        xcb_free = [p3b_end, p3b_end]
        hg_free = [p3b_end, p3b_end]
        yt_free = [[], []]
        for t in range(NT):
            par = t % 2
            pr.burst(q_ld)
            pr.dma("sync", xcb[par], x2_h[t * P:(t + 1) * P, :],
                   waits=xcb_free[par], inc=q_ld)
            pr.dma("sync", IX[:, 24 + par * 2: 26 + par * 2],
                   adr_h[t * P:(t + 1) * P, :],
                   waits=xcb_free[par] + [adr_all], inc=q_ld)
            c_ready = (q_ld, q_ld.n)
            pr.burst(q_ix)
            for (dst, col) in ((hga, 24 + par * 2), (hgb, 25 + par * 2)):
                pr.emit("gpsimd", lambda gp, dst=dst, col=col, par=par: gp.indirect_dma_start(
                    dst[par], None, hE_h,
                    bass.IndirectOffsetOnAxis(ap=IX[:, col:col + 1], axis=0)),
                    waits=[c_ready, hE_done, (q_ix, q_ix.mark)] + hg_free[par],
                    inc=q_ix, amt=16)
            g_ready2 = (q_ix, q_ix.n)
            pr.emit("vector", lambda v, par=par: v.tensor_tensor(
                yt[par], xcb[par], hga[par], ALU.add),
                waits=[g_ready2] + yt_free[par], inc=s_v)
            pr.emit("vector", lambda v, par=par: v.tensor_tensor(
                yt[par], yt[par], hgb[par], ALU.add), inc=s_v)
            y_ready = (s_v, s_v.n)
            xcb_free[par] = [(s_v, s_v.n)]
            hg_free[par] = [(s_v, s_v.n)]
            pr.burst(q_st)
            pr.dma("gpsimd", y_h[t * P:(t + 1) * P, :], yt[par],
                   waits=[y_ready], inc=q_st)
            yt_free[par] = [(q_st, q_st.n)]

    for sem in (q_st, q_ix, q_sc):
        if sem.n > 0:
            pr.emit("gpsimd", lambda g, sem=sem: g.wait_ge(sem.h, sem.n))
            pr.emit("sync", lambda s, sem=sem: s.wait_ge(sem.h, sem.n))

    with nc.Block() as block:
        @block.sync
        def _(sync):
            pr.replay("sync", sync)

        @block.tensor
        def _(tensor):
            pr.replay("tensor", tensor)

        @block.vector
        def _(vector):
            pr.replay("vector", vector)

        @block.scalar
        def _(scalar):
            pr.replay("scalar", scalar)

        @block.gpsimd
        def _(gpsimd):
            pr.replay("gpsimd", gpsimd)

    return nc


def make_in_map(xb, wq, wk, wv, gw, ew, eb):
    return {
        "x": np.ascontiguousarray(xb),
        "wq": np.ascontiguousarray(wq),
        "wk": np.ascontiguousarray(wk),
        "wv": np.ascontiguousarray(wv),
        "gw": np.ascontiguousarray(gw),
        "ew": np.ascontiguousarray(ew),
        "eb": np.ascontiguousarray(eb),
        "ident": np.eye(P, dtype=np.float32),
        "onesr": np.ones((1, P), dtype=np.float32),
        "ltri": np.triu(np.ones((P, P), dtype=np.float32), 1),
        "onesc": np.ones((P, 1), dtype=np.float32),
        "iotaE": np.broadcast_to(np.arange(E, dtype=np.float32), (P, E)).copy(),
        "iotaEB": np.broadcast_to(np.arange(E, dtype=np.float32) + BIGC, (P, E)).copy(),
        "iota128": np.arange(P, dtype=np.float32).reshape(P, 1),
    }


def kernel(L0_attn_x, L0_attn_W_query, L0_attn_W_key, L0_attn_W_value,
           L0_ffn_gate_w, L0_ffn_expert_w, L0_ffn_expert_b):
    x = np.ascontiguousarray(np.asarray(L0_attn_x, dtype=np.float32))
    wq = np.asarray(L0_attn_W_query, dtype=np.float32)
    wk = np.asarray(L0_attn_W_key, dtype=np.float32)
    wv = np.asarray(L0_attn_W_value, dtype=np.float32)
    gw = np.asarray(L0_ffn_gate_w, dtype=np.float32)
    ew = np.asarray(L0_ffn_expert_w, dtype=np.float32)
    eb = np.asarray(L0_ffn_expert_b, dtype=np.float32)

    nc = build()
    in_maps = [make_in_map(x[b], wq, wk, wv, gw, ew, eb) for b in range(8)]
    res = run_bass_kernel_spmd(nc, in_maps, core_ids=list(range(8)))
    y = np.stack([np.asarray(res.results[b]["y"]) for b in range(8)], axis=0)
    return y.astype(np.float32)


# version 23
# speedup vs baseline: 1.1414x; 1.0162x over previous
import numpy as np
import concourse.bass as bass
import concourse.mybir as mybir
from concourse.bass_utils import run_bass_kernel_spmd

F32 = mybir.dt.float32
BF16 = mybir.dt.bfloat16
I32 = mybir.dt.int32
AF = mybir.ActivationFunctionType
ALU = mybir.AluOpType
AX = mybir.AxisListType

P = 128
D = 1024
S = 4096
E = 16
DC = D // P
NT = S // P
CAP = 768
GT_N = CAP // P
BIGC = 65536.0
ARENA_W = 52736

ENGINES = ("sync", "tensor", "vector", "scalar", "gpsimd")


class Prog:

    def __init__(self, nc):
        self.nc = nc
        self.ops = {e: [] for e in ENGINES}
        self._cms = []

    def sem(self, name):
        cm = self.nc.semaphore(name)
        h = cm.__enter__()
        self._cms.append(cm)

        class Sem:
            def __init__(s):
                s.h = h
                s.n = 0
                s.mark = 0
        return Sem()

    def emit(self, eng, fn, waits=(), inc=None, amt=1, drain=True):
        ws = tuple((s.h, v) for (s, v) in waits if v > 0)
        self.ops[eng].append((ws, fn, inc.h if inc is not None else None, amt, drain))
        if inc is not None:
            inc.n += amt
            return (inc, inc.n)
        return None

    def dma(self, eng, out_ap, in_ap, waits=(), inc=None, slow_ok=False):
        def fn(e, o=out_ap, i=in_ap, s=slow_ok):
            return e.dma_start(out=o, in_=i, allow_slow_non_contiguous=s)
        ws = list(waits)
        if inc is not None:
            ws.append((inc, inc.mark))
        return self.emit(eng, fn, waits=ws, inc=inc, amt=16)

    @staticmethod
    def burst(sem):
        sem.mark = sem.n

    def replay(self, eng_name, eng):
        drains = eng_name in ("vector", "scalar")
        first = True
        waited = {}
        for ws, fn, inc_h, amt, drain in self.ops[eng_name]:
            if drains and drain and not first:
                eng.drain()
            first = False
            for h, v in ws:
                if waited.get(h, -1) < v:
                    eng.wait_ge(h, v)
                    waited[h] = v
            instr = fn(eng)
            if inc_h is not None:
                instr.then_inc(inc_h, amt)


def build(phases=("p1", "pw", "p2", "p3"), debug_outputs=False):
    nc = bass.Bass("TRN2", target_bir_lowering=False)

    x_h = nc.dram_tensor("x", [S, D], F32, kind="ExternalInput").ap()
    wq_h = nc.dram_tensor("wq", [D, D], F32, kind="ExternalInput").ap()
    wk_h = nc.dram_tensor("wk", [D, D], F32, kind="ExternalInput").ap()
    wv_h = nc.dram_tensor("wv", [D, D], F32, kind="ExternalInput").ap()
    gw_h = nc.dram_tensor("gw", [E, D], F32, kind="ExternalInput").ap()
    ew_h = nc.dram_tensor("ew", [E, D, D], F32, kind="ExternalInput").ap()
    eb_h = nc.dram_tensor("eb", [E, D], F32, kind="ExternalInput").ap()
    id_h = nc.dram_tensor("ident", [P, P], F32, kind="ExternalInput").ap()
    on_h = nc.dram_tensor("onesr", [1, P], F32, kind="ExternalInput").ap()
    lt_h = nc.dram_tensor("ltri", [P, P], F32, kind="ExternalInput").ap()
    oc_h = nc.dram_tensor("onesc", [P, 1], F32, kind="ExternalInput").ap()
    ie_h = nc.dram_tensor("iotaE", [P, E], F32, kind="ExternalInput").ap()
    ib_h = nc.dram_tensor("iotaEB", [P, E], F32, kind="ExternalInput").ap()
    i128_h = nc.dram_tensor("iota128", [P, 1], F32, kind="ExternalInput").ap()

    y_h = nc.dram_tensor("y", [S, D], F32, kind="ExternalOutput").ap()

    sk = "ExternalOutput" if debug_outputs else "Internal"
    qTh_h = nc.dram_tensor("qTh", [D, S], BF16, kind=sk).ap()
    qTl_h = nc.dram_tensor("qTl", [D, S], BF16, kind=sk).ap()
    kTh_h = nc.dram_tensor("kTh", [D, S], BF16, kind=sk).ap()
    kTl_h = nc.dram_tensor("kTl", [D, S], BF16, kind=sk).ap()
    vh_h = nc.dram_tensor("vh", [S, D], BF16, kind=sk).ap()
    vl_h = nc.dram_tensor("vl", [S, D], BF16, kind=sk).ap()
    wTh_h = nc.dram_tensor("wTh", [E, D, D], BF16, kind="Internal").ap()
    wTl_h = nc.dram_tensor("wTl", [E, D, D], BF16, kind="Internal").ap()
    x2_h = nc.dram_tensor("x2", [S, D], F32, kind=sk).ap()
    tlnT_h = nc.dram_tensor("tlnT", [D, S], F32, kind=sk).ap()
    tln_h = nc.dram_tensor("tln", [S, D], F32, kind="Internal").ap()
    toks_h = nc.dram_tensor("toks", [E * CAP], I32, kind="Internal").ap()
    ws_h = nc.dram_tensor("ws", [E * CAP], F32, kind="Internal").ap()
    hE_h = nc.dram_tensor("hE", [E * CAP, D], F32, kind="Internal").ap()
    adr_h = nc.dram_tensor("adr", [S, 2], I32, kind="Internal").ap()

    arena_cm = nc.sbuf_tensor("arena", [P, ARENA_W], F32)
    AR = arena_cm.__enter__()
    ARB = AR.bitcast(BF16)
    off = [0]

    def alloc(w):
        o = off[0]
        assert o + w <= ARENA_W, f"arena overflow at {o}+{w}"
        off[0] = o + w
        return o

    def view(o, w, p0=0, pn=P):
        return AR[p0:p0 + pn, o:o + w]

    def bpair(w):
        o = alloc(w)
        return (ARB[:, 2 * o:2 * o + w], ARB[:, 2 * o + w:2 * o + 2 * w])

    o_ident = alloc(128)
    o_ones = alloc(128)
    o_cols = alloc(128)
    o_gwT = alloc(128)
    o_ltri = alloc(128)
    o_onesc = alloc(1)
    o_iotaE = alloc(E)
    o_iotaEB = alloc(E)
    o_i128 = alloc(1)
    o_rows = alloc(64)
    off[0] = (off[0] + 127) // 128 * 128
    o_base = off[0]

    ident = view(o_ident, 128)
    ones_r = view(o_ones, 128, 0, 1)
    gwT = view(o_gwT, 128)
    ltri = view(o_ltri, 128)
    onesc = view(o_onesc, 1)
    iotaE = view(o_iotaE, E)
    iotaEB = view(o_iotaEB, E)
    i128c = view(o_i128, 1)

    def rows(i, n=1):
        return view(o_rows + i, n, 0, 1)

    def cols(i, n=1):
        return view(o_cols + i, n)

    def region():
        off[0] = o_base

    ix_cm = nc.sbuf_tensor("ixbuf", [P, 28], I32)
    IX = ix_cm.__enter__()

    pb_cms = [nc.psum_tensor(f"pb{i}", [P, 512], F32) for i in range(8)]
    pb = [cm.__enter__() for cm in pb_cms]

    pr = Prog(nc)
    q_ld = pr.sem("q_ld")
    q_st = pr.sem("q_st")
    q_ix = pr.sem("q_ix")
    q_sc = pr.sem("q_sc")
    s_t = pr.sem("s_t")
    s_v = pr.sem("s_v")
    s_s = pr.sem("s_s")
    allsems = (q_ld, q_st, q_ix, q_sc, s_t, s_v, s_s)

    def snap():
        return [(s, s.n) for s in allsems if s.n > 0]

    bfree = {i: [] for i in range(8)}

    def mm3(b, shi, slo, mhi, mlo, start, stop, waits=(), inc=None):
        pr.emit("tensor", lambda t_, b=b, shi=shi, mhi=mhi, start=start: t_.matmul(
            pb[b][:, :], shi, mhi, start=start, stop=False), waits=waits)
        pr.emit("tensor", lambda t_, b=b, shi=shi, mlo=mlo: t_.matmul(
            pb[b][:, :], shi, mlo, start=False, stop=False))
        return pr.emit("tensor", lambda t_, b=b, slo=slo, mhi=mhi, stop=stop: t_.matmul(
            pb[b][:, :], slo, mhi, start=False, stop=stop), inc=inc)

    def split_psum(b, hi_ap, lo_ap, scale, hi_waits=(), lo_waits=()):
        pr.emit("scalar", lambda s, b=b, hi_ap=hi_ap, scale=scale: s.activation(
            hi_ap, pb[b][:, :], AF.Copy, scale=scale),
            waits=list(hi_waits), inc=s_s)
        pr.emit("vector", lambda v, b=b, hi_ap=hi_ap, lo_ap=lo_ap, scale=scale: v.scalar_tensor_tensor(
            lo_ap, pb[b][:, :], scale, hi_ap, ALU.mult, ALU.subtract),
            waits=[(s_s, s_s.n)] + list(lo_waits), inc=s_v, drain=False)
        return (s_v, s_v.n)

    def split_sbuf(src_ap, hi_ap, lo_ap, waits=()):
        pr.emit("scalar", lambda s, src_ap=src_ap, hi_ap=hi_ap: s.activation(
            hi_ap, src_ap, AF.Copy), waits=list(waits), inc=s_s)
        pr.emit("vector", lambda v, src_ap=src_ap, hi_ap=hi_ap, lo_ap=lo_ap: v.tensor_tensor(
            lo_ap, src_ap, hi_ap, ALU.subtract),
            waits=[(s_s, s_s.n)], inc=s_v, drain=False)
        return (s_v, s_v.n)

    pr.dma("sync", ident, id_h, inc=q_ld)
    pr.dma("sync", ones_r, on_h, inc=q_ld)
    pr.dma("sync", ltri, lt_h, inc=q_ld)
    pr.dma("sync", onesc, oc_h, inc=q_ld)
    pr.dma("sync", iotaE, ie_h, inc=q_ld)
    pr.dma("sync", iotaEB, ib_h, inc=q_ld)
    pr.dma("sync", i128c, i128_h, inc=q_ld)
    region()
    wqp = bpair(DC * D)
    wkp = bpair(DC * D)
    wvp = bpair(DC * D)
    wstg = [view(alloc(D), D) for _ in range(2)]
    xbuf = view(alloc(4 * D), 4 * D)
    xln = view(alloc(4 * D), 4 * D)
    xlnTp = bpair(DC * 512)
    qTbp = bpair(4 * D)
    kTbp = bpair(4 * D)
    o_vbufp = off[0]
    vbufp = bpair(4 * D)
    gwsrc = view(o_vbufp, D, 0, E)

    pr.burst(q_ld)
    pr.dma("sync", xbuf.rearrange("p (t d) -> p t d", t=4),
           x_h[0:512, :].rearrange("(t p) d -> p t d", p=P), inc=q_ld)
    xld_tok = (q_ld, q_ld.n)

    wstg_free = [[], []]
    w_done = {}
    for mi, (nm, src_h, pair) in enumerate((("q", wq_h, wqp), ("k", wk_h, wkp),
                                            ("v", wv_h, wvp))):
        for c in range(DC):
            par = (mi * DC + c) % 2
            pr.burst(q_ld)
            pr.dma("sync", wstg[par], src_h[c * P:(c + 1) * P, :],
                   waits=wstg_free[par], inc=q_ld)
            tok = split_sbuf(wstg[par], pair[0][:, c * D:(c + 1) * D],
                             pair[1][:, c * D:(c + 1) * D],
                             waits=[(q_ld, q_ld.n)])
            wstg_free[par] = [tok]
        w_done[nm] = tok

    pr.burst(q_ld)
    pr.dma("sync", gwsrc, gw_h, inc=q_ld)
    ld0 = (q_ld, q_ld.n)

    for c in range(DC):
        pr.emit("tensor",
                lambda t, c=c: t.transpose(pb[0][:, c * E:(c + 1) * E],
                                           gwsrc[:, c * P:(c + 1) * P],
                                           ident[0:E, 0:E]),
                waits=[ld0], inc=s_t if c == DC - 1 else None)
    pr.emit("vector", lambda v: v.tensor_copy(gwT, pb[0][:, 0:P]),
            waits=[(s_t, s_t.n)], inc=s_v)
    gwT_ready = (s_v, s_v.n)
    bfree[0] = [gwT_ready]

    def ln_tiles(xsrc, xdst, ntiles, wait_ready, extra_apply_waits=()):
        for t in range(ntiles):
            sl = (slice(None), slice(t * D, (t + 1) * D))
            pr.emit("vector", lambda v, t=t, sl=sl: v.reduce_sum(
                cols(0 + t), xsrc[sl], axis=AX.X), waits=[wait_ready])
            pr.emit("vector", lambda v, t=t: v.tensor_scalar(
                cols(0 + t), cols(0 + t), 1.0 / D, None, ALU.mult), inc=s_v)
        meang = (s_v, s_v.n)
        for t in range(ntiles):
            sl = (slice(None), slice(t * D, (t + 1) * D))
            pr.emit("vector", lambda v, t=t, sl=sl: v.scalar_tensor_tensor(
                xdst[sl], xsrc[sl], cols(0 + t), xsrc[sl], ALU.subtract, ALU.mult,
                accum_out=cols(4 + t)), waits=[meang], inc=s_v)
        for t in range(ntiles):
            pr.emit("vector", lambda v, t=t: v.tensor_scalar(
                cols(4 + t), cols(4 + t), 1.0 / D, 1e-5, ALU.mult, ALU.add),
                inc=s_v)
        varg = (s_v, s_v.n)
        for t in range(ntiles):
            pr.emit("scalar", lambda s, t=t: s.activation(
                cols(8 + t), cols(4 + t), AF.Sqrt),
                waits=[varg], inc=s_s)
        sqg = (s_s, s_s.n)
        for t in range(ntiles):
            pr.emit("vector", lambda v, t=t: v.reciprocal(
                cols(12 + t), cols(8 + t)), waits=[sqg], inc=s_v)
        vg = (s_v, s_v.n)
        for t in range(ntiles):
            sl = (slice(None), slice(t * D, (t + 1) * D))
            pr.emit("vector", lambda v, t=t, sl=sl: v.tensor_scalar(
                xdst[sl], xsrc[sl], cols(0 + t), cols(12 + t),
                ALU.subtract, ALU.mult),
                waits=[vg] + list(extra_apply_waits), inc=s_v)

    if "p1" in phases:
        xbuf_free = []
        xlnT_wfree = []
        qTb_free, kTb_free, vbuf_free = [], [], []
        for nb in range(8):
            ln_tiles(xbuf, xln, 4, xld_tok)
            xln_ready = (s_v, s_v.n)
            xbuf_free = [(s_v, s_v.n)]
            for c in range(DC):
                b = c % 2
                for t in range(4):
                    pr.emit("tensor", lambda t_, c=c, t2=t, b=b: t_.transpose(
                        pb[b][:, t2 * P:(t2 + 1) * P],
                        xln[:, t2 * D + c * P: t2 * D + (c + 1) * P], ident),
                        waits=[xln_ready] + bfree[b],
                        inc=s_t if t == 3 else None)
                tok = split_psum(b, xlnTp[0][:, c * 512:(c + 1) * 512],
                                 xlnTp[1][:, c * 512:(c + 1) * 512], 1.0,
                                 hi_waits=[(s_t, s_t.n)] + xlnT_wfree,
                                 lo_waits=xlnT_wfree)
                bfree[b] = [tok]
            xlnT_ready = tok
            for proj in ("q", "k"):
                wpair = wqp if proj == "q" else wkp
                blkp = qTbp if proj == "q" else kTbp
                scale = (1.0 / 32.0) if proj == "q" else 1.0
                free_tok = qTb_free if proj == "q" else kTb_free
                for ec in range(DC):
                    b = 2 + ec % 2
                    for c in range(DC):
                        mm3(b, wpair[0][:, c * D + ec * P: c * D + (ec + 1) * P],
                            wpair[1][:, c * D + ec * P: c * D + (ec + 1) * P],
                            xlnTp[0][:, c * 512:(c + 1) * 512],
                            xlnTp[1][:, c * 512:(c + 1) * 512],
                            start=(c == 0), stop=(c == DC - 1),
                            waits=([xlnT_ready, w_done[proj]] + bfree[b]) if c == 0 else (),
                            inc=s_t if c == DC - 1 else None)
                    tok = split_psum(b, blkp[0][:, ec * 512:(ec + 1) * 512],
                                     blkp[1][:, ec * 512:(ec + 1) * 512], scale,
                                     hi_waits=[(s_t, s_t.n)] + free_tok,
                                     lo_waits=free_tok)
                    bfree[b] = [tok]
                sc_done = tok
                hdst = qTh_h if proj == "q" else kTh_h
                ldst = qTl_h if proj == "q" else kTl_h
                pr.burst(q_st)
                pr.dma("gpsimd", hdst.rearrange("(c p) n -> p c n", p=P)[:, :, nb * 512:(nb + 1) * 512],
                       blkp[0].rearrange("p (c n) -> p c n", c=DC), waits=[sc_done], inc=q_st)
                pr.dma("gpsimd", ldst.rearrange("(c p) n -> p c n", p=P)[:, :, nb * 512:(nb + 1) * 512],
                       blkp[1].rearrange("p (c n) -> p c n", c=DC), waits=[sc_done], inc=q_st)
                if proj == "q":
                    qTb_free = [(q_st, q_st.n)]
                else:
                    kTb_free = [(q_st, q_st.n)]
            pr.burst(q_st)
            for t in range(4):
                for eh in range(2):
                    b = 4 + (t * 2 + eh) % 2
                    for c in range(DC):
                        mm3(b, xlnTp[0][:, c * 512 + t * P: c * 512 + (t + 1) * P],
                            xlnTp[1][:, c * 512 + t * P: c * 512 + (t + 1) * P],
                            wvp[0][:, c * D + eh * 512: c * D + (eh + 1) * 512],
                            wvp[1][:, c * D + eh * 512: c * D + (eh + 1) * 512],
                            start=(c == 0), stop=(c == DC - 1),
                            waits=([xlnT_ready, w_done["v"]] + bfree[b]) if c == 0 else (),
                            inc=s_t if c == DC - 1 else None)
                    tok = split_psum(b, vbufp[0][:, t * D + eh * 512: t * D + (eh + 1) * 512],
                                     vbufp[1][:, t * D + eh * 512: t * D + (eh + 1) * 512], 1.0,
                                     hi_waits=[(s_t, s_t.n)] + vbuf_free,
                                     lo_waits=vbuf_free)
                    bfree[b] = [tok]
                pr.dma("gpsimd", vh_h[(nb * 4 + t) * P:(nb * 4 + t + 1) * P, :],
                       vbufp[0][:, t * D:(t + 1) * D], waits=[tok], inc=q_st)
                pr.dma("gpsimd", vl_h[(nb * 4 + t) * P:(nb * 4 + t + 1) * P, :],
                       vbufp[1][:, t * D:(t + 1) * D], waits=[tok], inc=q_st)
            vbuf_free = [(q_st, q_st.n)]
            xlnT_wfree = [(s_t, s_t.n)]
            if nb < 7:
                pr.burst(q_ld)
                pr.dma("sync", xbuf.rearrange("p (t d) -> p t d", t=4),
                       x_h[(nb + 1) * 512:(nb + 2) * 512, :].rearrange(
                           "(t p) d -> p t d", p=P),
                       waits=xbuf_free, inc=q_ld)
                xld_tok = (q_ld, q_ld.n)

    p1_end = snap()
    pw_end = p1_end

    if "p2" in phases:
        region()
        kTgp = [bpair(2048) for _ in range(2)]
        qTb2p = [bpair(4096) for _ in range(2)]
        vb2p = [bpair(4 * D) for _ in range(2)]
        PTbp = [bpair(4096) for _ in range(2)]
        Sb = [view(alloc(4 * D), 4 * D) for _ in range(2)]
        Pb = [view(alloc(4 * D), 4 * D) for _ in range(2)]
        pmax = [view(alloc(16), 16) for _ in range(2)]
        xres = view(alloc(D), D)
        x2t = view(alloc(D), D)
        x2ln = view(alloc(D), D)
        tlnTb = view(alloc(D), D)
        ewstg = [view(alloc(512), 512) for _ in range(2)]
        ewpp = [bpair(512) for _ in range(2)]
        ewstg_free = [pw_end, pw_end]
        ewpp_free = [pw_end, pw_end]
        ew_ld_tok = [None, None]

        def ew_split1(e, k):
            par = k % 2
            c = k // 2
            hf = k % 2
            tok = split_sbuf(ewstg[par], ewpp[par][0], ewpp[par][1],
                             waits=[ew_ld_tok[par]] + ewpp_free[par])
            ewstg_free[par] = [tok]
            pr.burst(q_st)
            pr.dma("gpsimd", wTh_h[e][c * P:(c + 1) * P, hf * 512:(hf + 1) * 512],
                   ewpp[par][0], waits=[tok], inc=q_st)
            pr.dma("gpsimd", wTl_h[e][c * P:(c + 1) * P, hf * 512:(hf + 1) * 512],
                   ewpp[par][1], waits=[tok], inc=q_st)
            ewpp_free[par] = [(q_st, q_st.n)]

        def ew_pt(e, k):
            par = k % 2
            c = k // 2
            hf = k % 2
            pr.burst(q_sc)
            pr.dma("gpsimd", ewstg[par],
                   ew_h[e][c * P:(c + 1) * P, hf * 512:(hf + 1) * 512],
                   waits=ewstg_free[par], inc=q_sc)
            ew_ld_tok[par] = (q_sc, q_sc.n)
            if k >= 1:
                ew_split1(e, k - 1)

        kTg_free = [pw_end, pw_end]
        qTb2_free = [pw_end, pw_end]
        vb2_free = [pw_end, pw_end]
        P_free = [pw_end, pw_end]
        PT_free = [pw_end, pw_end]
        x2t_free = []
        x2ln_free = []
        tlnTb_free = []
        for i in range(8):
            bfree[i] = bfree[i] + pw_end

        def ln_one(xsrc, xdst, wait_ready, extra_apply_waits=()):
            pr.emit("vector", lambda v: v.reduce_sum(cols(0), xsrc, axis=AX.X),
                    waits=[wait_ready])
            pr.emit("vector", lambda v: v.tensor_scalar(
                cols(0), cols(0), 1.0 / D, None, ALU.mult), inc=s_v)
            g1 = (s_v, s_v.n)
            pr.emit("vector", lambda v: v.scalar_tensor_tensor(
                xdst, xsrc, cols(0), xsrc, ALU.subtract, ALU.mult,
                accum_out=cols(1)), waits=[g1] + list(extra_apply_waits), inc=s_v)
            pr.emit("vector", lambda v: v.tensor_scalar(
                cols(1), cols(1), 1.0 / D, 1e-5, ALU.mult, ALU.add), inc=s_v)
            pr.emit("scalar", lambda s: s.activation(
                cols(2), cols(1), AF.Sqrt),
                waits=[(s_v, s_v.n)], inc=s_s)
            pr.emit("vector", lambda v: v.reciprocal(cols(3), cols(2)),
                    waits=[(s_s, s_s.n)], inc=s_v)
            g2 = (s_v, s_v.n)
            pr.emit("vector", lambda v: v.tensor_scalar(
                xdst, xsrc, cols(0), cols(3), ALU.subtract, ALU.mult),
                waits=[g2] + list(extra_apply_waits), inc=s_v)

        for jg in range(16):
            kpar = jg % 2
            pr.burst(q_ld)
            pr.dma("sync", kTgp[kpar][0].rearrange("p (c n) -> p c n", c=DC),
                   kTh_h.rearrange("(c p) n -> p c n", p=P)[:, :, jg * 256:(jg + 1) * 256],
                   waits=kTg_free[kpar], inc=q_ld)
            pr.dma("sync", kTgp[kpar][1].rearrange("p (c n) -> p c n", c=DC),
                   kTl_h.rearrange("(c p) n -> p c n", p=P)[:, :, jg * 256:(jg + 1) * 256],
                   waits=kTg_free[kpar], inc=q_ld)
            kTg_ready = (q_ld, q_ld.n)
            for ib in range(8):
                par = ib % 2
                pr.burst(q_ld)
                pr.dma("sync", qTb2p[par][0].rearrange("p (c n) -> p c n", c=DC),
                       qTh_h.rearrange("(c p) n -> p c n", p=P)[:, :, ib * 512:(ib + 1) * 512],
                       waits=qTb2_free[par], inc=q_ld)
                pr.dma("sync", qTb2p[par][1].rearrange("p (c n) -> p c n", c=DC),
                       qTl_h.rearrange("(c p) n -> p c n", p=P)[:, :, ib * 512:(ib + 1) * 512],
                       waits=qTb2_free[par], inc=q_ld)
                q_ready = (q_ld, q_ld.n)
                for jt in range(2):
                    b = (ib * 2 + jt) % 2
                    for ec in range(DC):
                        mm3(b, kTgp[kpar][0][:, ec * 256 + jt * P: ec * 256 + (jt + 1) * P],
                            kTgp[kpar][1][:, ec * 256 + jt * P: ec * 256 + (jt + 1) * P],
                            qTb2p[par][0][:, ec * 512:(ec + 1) * 512],
                            qTb2p[par][1][:, ec * 512:(ec + 1) * 512],
                            start=(ec == 0), stop=(ec == DC - 1),
                            waits=([kTg_ready, q_ready] + bfree[b]) if ec == 0 else (),
                            inc=s_t if ec == DC - 1 else None)
                    pr.emit("scalar", lambda s, ib=ib, jt=jt, b=b: s.activation(
                        Sb[jt][:, ib * 512:(ib + 1) * 512], pb[b][:, :], AF.Copy, scale=1.0),
                        waits=[(s_t, s_t.n)], inc=s_s)
                    bfree[b] = [(s_s, s_s.n)]
                    pr.emit("vector", lambda v, ib=ib, jt=jt: v.reduce_max(
                        pmax[jt][:, ib:ib + 1], Sb[jt][:, ib * 512:(ib + 1) * 512],
                        axis=AX.X), waits=[(s_s, s_s.n)], inc=s_v, drain=False)
                qTb2_free[par] = [(s_t, s_t.n)]
                ew_pt(jg, ib)
            s_done = (s_s, s_s.n)
            exp_tok = [None, None]
            for jt in range(2):
                pr.emit("vector", lambda v, jt=jt: v.reduce_max(
                    cols(16 + jt), pmax[jt], axis=AX.X, negate=True),
                    waits=[(s_v, s_v.n)], inc=s_v)
                pr.emit("scalar", lambda s, jt=jt: s.activation(
                    Pb[jt], Sb[jt], AF.Exp, bias=cols(16 + jt),
                    accum_out=cols(18 + jt)),
                    waits=[(s_v, s_v.n)] + P_free[jt], inc=s_s)
                exp_tok[jt] = (s_s, s_s.n)
                pr.emit("vector", lambda v, jt=jt: v.reciprocal(
                    cols(20 + jt), cols(18 + jt)), waits=[(s_s, s_s.n)], inc=s_v)
            inv_ready = (s_v, s_v.n)
            sp_tok = [[None] * 8, [None] * 8]

            def emit_pt(g):
                for jt in range(2):
                    b = 2 + (g * 2 + jt) % 2
                    for k in range(4):
                        ic = g * 4 + k
                        pr.emit("tensor", lambda t_, ic=ic, k=k, jt=jt, b=b: t_.transpose(
                            pb[b][:, k * P:(k + 1) * P],
                            Pb[jt][:, ic * P:(ic + 1) * P], ident),
                            waits=[exp_tok[jt]] + bfree[b],
                            inc=s_t if k == 3 else None)
                    tok = split_psum(b, PTbp[jt][0][:, g * 512:(g + 1) * 512],
                                     PTbp[jt][1][:, g * 512:(g + 1) * 512], 1.0,
                                     hi_waits=[(s_t, s_t.n)] + PT_free[jt],
                                     lo_waits=PT_free[jt])
                    sp_tok[jt][g] = tok
                    bfree[b] = [tok]

            emit_pt(0)
            for vb in range(8):
                if vb + 1 < 8:
                    emit_pt(vb + 1)
                par = vb % 2
                pr.burst(q_ld)
                pr.dma("sync", vb2p[par][0].rearrange("p (cc d) -> p cc d", cc=4),
                       vh_h[vb * 512:(vb + 1) * 512, :].rearrange("(cc p) d -> p cc d", p=P),
                       waits=vb2_free[par], inc=q_ld)
                pr.dma("sync", vb2p[par][1].rearrange("p (cc d) -> p cc d", cc=4),
                       vl_h[vb * 512:(vb + 1) * 512, :].rearrange("(cc p) d -> p cc d", p=P),
                       waits=vb2_free[par], inc=q_ld)
                v_ready = (q_ld, q_ld.n)
                for cc in range(4):
                    ic = vb * 4 + cc
                    for jt in range(2):
                        for eh in range(2):
                            b = 4 + jt * 2 + eh
                            last = (cc == 3 and jt == 1 and eh == 1)
                            mm3(b, PTbp[jt][0][:, ic * P:(ic + 1) * P],
                                PTbp[jt][1][:, ic * P:(ic + 1) * P],
                                vb2p[par][0][:, cc * D + eh * 512: cc * D + (eh + 1) * 512],
                                vb2p[par][1][:, cc * D + eh * 512: cc * D + (eh + 1) * 512],
                                start=(ic == 0), stop=(ic == 31),
                                waits=[sp_tok[0][vb], sp_tok[1][vb], v_ready]
                                      + (bfree[b] if ic == 0 else []),
                                inc=s_t if last else None)
                vb2_free[par] = [(s_t, s_t.n)]
                ew_pt(jg, 8 + vb)
            ew_split1(jg, 15)
            for jt in range(2):
                P_free[jt] = [(s_t, s_t.n)]
            av_done = (s_t, s_t.n)
            PT_free = [[av_done], [av_done]]
            for jt in range(2):
                jr = jg * 2 + jt
                pr.burst(q_ld)
                pr.dma("sync", xres, x_h[jr * P:(jr + 1) * P, :],
                       waits=x2t_free, inc=q_ld)
                xr_ready = (q_ld, q_ld.n)
                for eh in range(2):
                    b = 4 + jt * 2 + eh
                    pr.emit("vector", lambda v, jt=jt, eh=eh, b=b: v.scalar_tensor_tensor(
                        x2t[:, eh * 512:(eh + 1) * 512], pb[b][:, :], cols(20 + jt),
                        xres[:, eh * 512:(eh + 1) * 512], ALU.mult, ALU.add),
                        waits=[av_done, xr_ready, inv_ready] + x2t_free, inc=s_v,
                        drain=False)
                    bfree[b] = [(s_v, s_v.n)]
                x2_ready = (s_v, s_v.n)
                pr.burst(q_st)
                pr.dma("gpsimd", x2_h[jr * P:(jr + 1) * P, :], x2t,
                       waits=[x2_ready], inc=q_st)
                ln_one(x2t, x2ln, x2_ready, extra_apply_waits=x2ln_free)
                ln2_done = (s_v, s_v.n)
                x2t_free = [(s_v, s_v.n), (q_st, q_st.n)]
                pr.burst(q_st)
                pr.dma("gpsimd", tln_h[jr * P:(jr + 1) * P, :], x2ln,
                       waits=[ln2_done], inc=q_st)
                tln_store_tok = (q_st, q_st.n)
                for half in range(2):
                    b = 2 + half
                    for k in range(4):
                        c = half * 4 + k
                        pr.emit("tensor", lambda t_, c=c, k=k, b=b: t_.transpose(
                            pb[b][:, k * P:(k + 1) * P],
                            x2ln[:, c * P:(c + 1) * P], ident),
                            waits=[ln2_done] + bfree[b],
                            inc=s_t if k == 3 else None)
                    pr.emit("vector", lambda v, half=half, b=b: v.tensor_copy(
                        tlnTb[:, half * 512:(half + 1) * 512], pb[b][:, :]),
                        waits=[(s_t, s_t.n)] + tlnTb_free, inc=s_v, drain=False)
                    bfree[b] = [(s_v, s_v.n)]
                x2ln_free = [(s_t, s_t.n), tln_store_tok]
                tb_ready = (s_v, s_v.n)
                pr.burst(q_st)
                pr.dma("gpsimd", tlnT_h.rearrange("(c p) n -> p c n", p=P)[:, :, jr * P:(jr + 1) * P],
                       tlnTb.rearrange("p (c n) -> p c n", c=DC), waits=[tb_ready], inc=q_st)
                tlnTb_free = [(q_st, q_st.n)]
            kTg_free[kpar] = [(s_t, s_t.n)]

    p2_end = snap()

    if "p3" in phases:
        region()
        tln = view(alloc(DC * D), DC * D)
        lgall = view(alloc(P), P)
        pall = view(alloc(P), P)
        cndall = view(alloc(P), P)
        geall = view(alloc(P), P)
        ohall1 = view(alloc(P), P)
        ohall2 = view(alloc(P), P)
        t8all = view(alloc(64), 64)
        sta = view(alloc(64), 64)
        pmd = view(alloc(E), E)
        zsb = view(alloc(E * CAP // P), E * CAP // P)

        p3a_start = p2_end
        slab_free = p3a_start

        pr.emit("vector", lambda v: v.memset(zsb, 0.0), waits=p3a_start, inc=s_v)
        zs_ready = (s_v, s_v.n)
        pr.burst(q_ix)
        pr.dma("gpsimd", toks_h.rearrange("(p g) -> p g", p=P), zsb,
               waits=[zs_ready], inc=q_ix)
        pr.dma("gpsimd", ws_h.rearrange("(p g) -> p g", p=P), zsb,
               waits=[zs_ready], inc=q_ix)
        zs_done = (q_ix, q_ix.n)

        pr.emit("vector", lambda v: v.memset(rows(0, 64), 0.0), inc=s_v)
        base_tok = (s_v, s_v.n)

        scat_done_prev = zs_done
        adr_done_prev = (q_st, 0)
        for slab in range(4):
            pr.burst(q_ld)
            pr.dma("sync", tln.rearrange("p (c n) -> p c n", c=DC),
                   tlnT_h.rearrange("(c p) n -> p c n", p=P)[:, :, slab * D:(slab + 1) * D],
                   waits=slab_free, inc=q_ld)
            slab_ready = (q_ld, q_ld.n)
            for ntile in range(8):
                for c in range(DC):
                    pr.emit("tensor", lambda t_, c=c, ntile=ntile: t_.matmul(
                        pb[0][:, ntile * E:(ntile + 1) * E],
                        tln[:, c * D + ntile * P: c * D + (ntile + 1) * P],
                        gwT[:, c * E:(c + 1) * E],
                        start=(c == 0), stop=(c == DC - 1)),
                        waits=([slab_ready, gwT_ready] + bfree[0]) if (ntile == 0 and c == 0) else (),
                        inc=s_t if (ntile == 7 and c == DC - 1) else None)
            pr.emit("scalar", lambda s: s.activation(
                lgall, pb[0][:, 0:P], AF.Copy, scale=1.0),
                waits=[(s_t, s_t.n)], inc=s_s)
            lg_tok = (s_s, s_s.n)
            bfree[0] = [lg_tok]
            for t in range(8):
                pr.emit("vector", lambda v, t=t: v.reduce_max(
                    sta[:, 0 + t:1 + t], lgall[:, t * E:(t + 1) * E], axis=AX.X,
                    negate=True),
                    waits=[lg_tok, scat_done_prev, adr_done_prev], inc=s_v)
            gmx_tok = (s_v, s_v.n)
            for t in range(8):
                pr.emit("scalar", lambda s, t=t: s.activation(
                    pall[:, t * E:(t + 1) * E], lgall[:, t * E:(t + 1) * E], AF.Exp,
                    bias=sta[:, 0 + t:1 + t], accum_out=sta[:, 8 + t:9 + t]),
                    waits=[gmx_tok], inc=s_s)
            exp_tok2 = (s_s, s_s.n)
            for t in range(8):
                pr.emit("vector", lambda v, t=t: v.reciprocal(
                    sta[:, 16 + t:17 + t], sta[:, 8 + t:9 + t]),
                    waits=[exp_tok2], inc=s_v)
            rec_tok = (s_v, s_v.n)
            for t in range(8):
                pr.emit("vector", lambda v, t=t: v.tensor_scalar(
                    pall[:, t * E:(t + 1) * E], pall[:, t * E:(t + 1) * E],
                    sta[:, 16 + t:17 + t], None, ALU.mult),
                    waits=[rec_tok], inc=s_v)
            for t in range(8):
                pr.emit("vector", lambda v, t=t: v.max(
                    t8all[:, t * 8:(t + 1) * 8], pall[:, t * E:(t + 1) * E]),
                    inc=s_v)
            t8_tok = (s_v, s_v.n)
            for t in range(8):
                pr.emit("vector", lambda v, t=t: v.tensor_scalar(
                    geall[:, t * E:(t + 1) * E], pall[:, t * E:(t + 1) * E],
                    t8all[:, t * 8:t * 8 + 1], None, ALU.is_ge),
                    waits=[t8_tok], inc=s_v)
            for t in range(8):
                pr.emit("vector", lambda v, t=t: v.scalar_tensor_tensor(
                    cndall[:, t * E:(t + 1) * E], geall[:, t * E:(t + 1) * E],
                    -BIGC, iotaEB, ALU.mult, ALU.add), inc=s_v)
            for t in range(8):
                pr.emit("vector", lambda v, t=t: v.tensor_reduce(
                    sta[:, 24 + t:25 + t], cndall[:, t * E:(t + 1) * E],
                    axis=AX.X, op=ALU.min), inc=s_v)
            e1_tok = (s_v, s_v.n)
            for t in range(8):
                pr.emit("vector", lambda v, t=t: v.tensor_scalar(
                    ohall1[:, t * E:(t + 1) * E], iotaE, sta[:, 24 + t:25 + t],
                    None, ALU.is_equal), waits=[e1_tok], inc=s_v)
            for t in range(8):
                pr.emit("vector", lambda v, t=t: v.tensor_scalar(
                    geall[:, t * E:(t + 1) * E], pall[:, t * E:(t + 1) * E],
                    t8all[:, t * 8 + 1:t * 8 + 2], None, ALU.is_ge), inc=s_v)
            for t in range(8):
                pr.emit("vector", lambda v, t=t: v.scalar_tensor_tensor(
                    cndall[:, t * E:(t + 1) * E], geall[:, t * E:(t + 1) * E],
                    -BIGC, iotaEB, ALU.mult, ALU.add), inc=s_v)
            oh1_tok = (s_v, s_v.n)
            for t in range(8):
                pr.emit("vector", lambda v, t=t: v.scalar_tensor_tensor(
                    cndall[:, t * E:(t + 1) * E], ohall1[:, t * E:(t + 1) * E],
                    BIGC, cndall[:, t * E:(t + 1) * E], ALU.mult, ALU.add),
                    waits=[oh1_tok], inc=s_v)
            for t in range(8):
                pr.emit("vector", lambda v, t=t: v.tensor_reduce(
                    sta[:, 32 + t:33 + t], cndall[:, t * E:(t + 1) * E],
                    axis=AX.X, op=ALU.min), inc=s_v)
            e2_tok = (s_v, s_v.n)
            for t in range(8):
                pr.emit("vector", lambda v, t=t: v.tensor_scalar(
                    ohall2[:, t * E:(t + 1) * E], iotaE, sta[:, 32 + t:33 + t],
                    None, ALU.is_equal), waits=[e2_tok], inc=s_v)
            oh_all_tok = (s_v, s_v.n)
            for ntile in range(8):
                gt = slab * 8 + ntile
                oh1 = ohall1[:, ntile * E:(ntile + 1) * E]
                oh2 = ohall2[:, ntile * E:(ntile + 1) * E]
                pr.emit("tensor", lambda t_, oh1=oh1: t_.matmul(
                    pb[1][:, 0:E], ltri, oh1, start=True, stop=False),
                    waits=[oh_all_tok, base_tok] + bfree[1] + bfree[2] + bfree[3])
                pr.emit("tensor", lambda t_: t_.matmul(
                    pb[1][:, 0:E], ones_r, rows(0, E), start=False, stop=True))
                pr.emit("tensor", lambda t_, oh1=oh1: t_.matmul(
                    pb[2][0:1, 0:E], onesc, oh1, start=True, stop=True))
                pr.emit("tensor", lambda t_, oh2=oh2: t_.matmul(
                    pb[3][:, 0:E], ltri, oh2, start=True, stop=False),
                    inc=s_t)
                mm1 = (s_t, s_t.n)
                pr.emit("vector", lambda v, oh1=oh1: v.scalar_tensor_tensor(
                    pmd, pb[1][:, 0:E], 1.0, oh1, ALU.mult, ALU.mult,
                    accum_out=cols(29)), waits=[mm1], inc=s_v)
                pr.emit("vector", lambda v: v.tensor_tensor(
                    rows(16, E), rows(0, E), pb[2][0:1, 0:E], ALU.add), inc=s_v)
                base_b_tok = (s_v, s_v.n)
                pr.emit("tensor", lambda t_: t_.matmul(
                    pb[3][:, 0:E], ones_r, rows(16, E), start=False, stop=True),
                    waits=[base_b_tok])
                pr.emit("tensor", lambda t_, oh2=oh2: t_.matmul(
                    pb[2][0:1, 16:32], onesc, oh2, start=True, stop=True),
                    inc=s_t)
                mm2 = (s_t, s_t.n)
                pr.emit("vector", lambda v, oh2=oh2: v.scalar_tensor_tensor(
                    pmd, pb[3][:, 0:E], 1.0, oh2, ALU.mult, ALU.mult,
                    accum_out=cols(30)), waits=[mm2], inc=s_v)
                pr.emit("vector", lambda v: v.tensor_tensor(
                    rows(0, E), rows(16, E), pb[2][0:1, 16:32], ALU.add), inc=s_v)
                base_tok = (s_v, s_v.n)
                bfree[1] = [base_tok]
                bfree[2] = [base_tok]
                bfree[3] = [base_tok]
                g4 = (s_v, s_v.n)
                pr.emit("vector", lambda v, ntile=ntile: v.tensor_scalar(
                    cols(72), sta[:, 24 + ntile:25 + ntile], float(CAP), cols(29),
                    ALU.mult, ALU.add), waits=[g4], inc=s_v)
                pr.emit("vector", lambda v, ntile=ntile: v.tensor_scalar(
                    cols(74), sta[:, 24 + ntile:25 + ntile], float(CAP),
                    float(CAP - 1), ALU.mult, ALU.add), inc=s_v)
                pr.emit("vector", lambda v: v.tensor_tensor(
                    cols(72), cols(72), cols(74), ALU.min), inc=s_v)
                pr.emit("vector", lambda v, ntile=ntile: v.tensor_scalar(
                    cols(73), sta[:, 32 + ntile:33 + ntile], float(CAP), cols(30),
                    ALU.mult, ALU.add), inc=s_v)
                pr.emit("vector", lambda v, ntile=ntile: v.tensor_scalar(
                    cols(74), sta[:, 32 + ntile:33 + ntile], float(CAP),
                    float(CAP - 1), ALU.mult, ALU.add), inc=s_v)
                pr.emit("vector", lambda v: v.tensor_tensor(
                    cols(73), cols(73), cols(74), ALU.min), inc=s_v)
                pr.emit("vector", lambda v, gt=gt: v.tensor_scalar(
                    cols(75), i128c, float(gt * P), None, ALU.add), inc=s_v)
                pr.emit("vector", lambda v: v.tensor_copy(IX[:, 24:25], cols(72)),
                        waits=[scat_done_prev, adr_done_prev], inc=s_v)
                pr.emit("vector", lambda v: v.tensor_copy(IX[:, 25:26], cols(73)), inc=s_v)
                pr.emit("vector", lambda v: v.tensor_copy(IX[:, 26:27], cols(75)), inc=s_v)
                route_ready = (s_v, s_v.n)
                pr.burst(q_st)
                pr.dma("gpsimd", adr_h[gt * P:(gt + 1) * P, :], IX[:, 24:26],
                       waits=[route_ready], inc=q_st)
                adr_done_prev = (q_st, q_st.n)
                pr.burst(q_ix)
                mark = (q_ix, q_ix.mark)
                toks2 = toks_h.rearrange("(a b) -> a b", b=1)
                ws2 = ws_h.rearrange("(a b) -> a b", b=1)
                for (ixc, vals) in ((24, IX[:, 26:27]), (25, IX[:, 26:27])):
                    pr.emit("gpsimd", lambda g, ixc=ixc, vals=vals, t2=toks2: g.indirect_dma_start(
                        t2, bass.IndirectOffsetOnAxis(ap=IX[:, ixc:ixc + 1], axis=0),
                        vals, None),
                        waits=[route_ready, mark], inc=q_ix, amt=16)
                for (ixc, tt) in ((24, 0), (25, 1)):
                    wv_ap = t8all[:, ntile * 8 + tt: ntile * 8 + tt + 1]
                    pr.emit("gpsimd", lambda g, ixc=ixc, wv_ap=wv_ap, w2=ws2: g.indirect_dma_start(
                        w2, bass.IndirectOffsetOnAxis(ap=IX[:, ixc:ixc + 1], axis=0),
                        wv_ap, None),
                        waits=[route_ready, mark], inc=q_ix, amt=16)
                scat_done_prev = (q_ix, q_ix.n)
            slab_free = [(s_t, s_t.n)]
        routing_done = (q_ix, q_ix.n)
        p3a_end = snap()

        region()
        Gb = [view(alloc(GT_N * D), GT_N * D) for _ in range(2)]
        GTp = bpair(DC * CAP)
        GT3h = GTp[0].rearrange("p (c n) -> p c n", c=DC)
        GT3l = GTp[1].rearrange("p (c n) -> p c n", c=DC)
        wtp = [bpair(DC * D) for _ in range(2)]
        wc = [view(alloc(8), 8) for _ in range(2)]
        ebr = [view(alloc(D), D, 0, 1) for _ in range(2)]
        bbc = [view(alloc(D), D) for _ in range(2)]
        t0b = [view(alloc(512), 512) for _ in range(2)]
        tE = [view(alloc(D), D) for _ in range(2)]

        wt_free = [p3a_end, p3a_end]
        eb_free = [p3a_end, p3a_end]
        wc_free = [p3a_end, p3a_end]
        idx_free = [p3a_end, p3a_end]
        bbc_free = [p3a_end, p3a_end]
        G_free = [p3a_end, p3a_end]
        GT_free = []
        t0_free = [[], []]
        tE_free = [[], []]
        w_ready = [None] * E
        g_ready = [None] * E

        def p3_loads(e):
            par = e % 2
            pr.burst(q_ld)
            for c in range(DC):
                pr.emit("sync", lambda s, e=e, c=c, par=par: s.dma_start(
                    out=wtp[par][0][:, c * D:(c + 1) * D],
                    in_=wTh_h[e][:, c * P:(c + 1) * P], transpose=True),
                    waits=wt_free[par] + [(q_ld, q_ld.mark)], inc=q_ld, amt=16)
                pr.emit("sync", lambda s, e=e, c=c, par=par: s.dma_start(
                    out=wtp[par][1][:, c * D:(c + 1) * D],
                    in_=wTl_h[e][:, c * P:(c + 1) * P], transpose=True),
                    waits=wt_free[par] + [(q_ld, q_ld.mark)], inc=q_ld, amt=16)
            pr.dma("sync", ebr[par], eb_h[e:e + 1, :], waits=eb_free[par], inc=q_ld)
            pr.dma("sync", wc[par][:, 0:GT_N],
                   ws_h[e * CAP:(e + 1) * CAP].rearrange("(g p) -> p g", p=P),
                   waits=wc_free[par] + [routing_done], inc=q_ld, slow_ok=True)
            pr.dma("sync", IX[:, par * 6 + 0: par * 6 + GT_N],
                   toks_h[e * CAP:(e + 1) * CAP].rearrange("(g p) -> p g", p=P),
                   waits=wc_free[par] + idx_free[par] + [routing_done], inc=q_ld,
                   slow_ok=True)
            w_ready[e] = (q_ld, q_ld.n)

        def p3_gathers(e):
            par = e % 2
            pr.burst(q_ix)
            for g in range(GT_N):
                pr.emit("gpsimd", lambda gp, g=g, par=par: gp.indirect_dma_start(
                    Gb[par][:, g * D:(g + 1) * D], None, tln_h,
                    bass.IndirectOffsetOnAxis(
                        ap=IX[:, par * 6 + g: par * 6 + g + 1], axis=0)),
                    waits=[w_ready[e], routing_done, (q_ix, q_ix.mark)] + G_free[par],
                    inc=q_ix, amt=16)
            g_ready[e] = (q_ix, q_ix.n)

        p3_loads(0)
        p3_gathers(0)
        for e in range(E):
            par = e % 2
            if e + 1 < E:
                p3_loads(e + 1)
                p3_gathers(e + 1)
            for eh in range(2):
                pr.emit("tensor", lambda t_, eh=eh, par=par: t_.matmul(
                    pb[4 + eh][:, :], ones_r, ebr[par][:, eh * 512:(eh + 1) * 512],
                    start=True, stop=True),
                    waits=[w_ready[e]] + bfree[4 + eh],
                    inc=s_t if eh == 1 else None)
            for eh in range(2):
                pr.emit("vector", lambda v, eh=eh, par=par: v.tensor_copy(
                    bbc[par][:, eh * 512:(eh + 1) * 512], pb[4 + eh][:, :]),
                    waits=[(s_t, s_t.n)] + bbc_free[par], inc=s_v, drain=False)
                bfree[4 + eh] = [(s_v, s_v.n)]
            bb_ready = (s_v, s_v.n)
            gt_tok = [None] * GT_N

            def emit_gt(g, par=par, e=e):
                for half in range(2):
                    b = 6 + (g * 2 + half) % 2
                    for k in range(4):
                        c = half * 4 + k
                        pr.emit("tensor", lambda t_, g=g, c=c, k=k, par=par, b=b: t_.transpose(
                            pb[b][:, k * P:(k + 1) * P],
                            Gb[par][:, g * D + c * P: g * D + (c + 1) * P], ident),
                            waits=[g_ready[e]] + bfree[b],
                            inc=s_t if k == 3 else None)
                    tok = split_psum(b, GT3h[:, half * 4:(half + 1) * 4, g * P:(g + 1) * P],
                                     GT3l[:, half * 4:(half + 1) * 4, g * P:(g + 1) * P], 1.0,
                                     hi_waits=[(s_t, s_t.n)] + GT_free,
                                     lo_waits=GT_free)
                    bfree[b] = [tok]
                gt_tok[g] = tok

            emit_gt(0)
            for g in range(GT_N):
                if g + 1 < GT_N:
                    emit_gt(g + 1)
                for eh in range(2):
                    bi = (g * 2 + eh) % 2
                    b = 2 + bi
                    for c in range(DC):
                        mm3(b, GTp[0][:, c * CAP + g * P: c * CAP + (g + 1) * P],
                            GTp[1][:, c * CAP + g * P: c * CAP + (g + 1) * P],
                            wtp[par][0][:, c * D + eh * 512: c * D + (eh + 1) * 512],
                            wtp[par][1][:, c * D + eh * 512: c * D + (eh + 1) * 512],
                            start=(c == 0), stop=(c == DC - 1),
                            waits=([gt_tok[g], w_ready[e]] + bfree[b]) if c == 0 else (),
                            inc=s_t if c == DC - 1 else None)
                    pr.emit("vector", lambda v, eh=eh, par=par, bi=bi, b=b: v.tensor_tensor(
                        t0b[bi], pb[b][:, :], bbc[par][:, eh * 512:(eh + 1) * 512],
                        ALU.add),
                        waits=[(s_t, s_t.n), bb_ready] + t0_free[bi], inc=s_v,
                        drain=False)
                    bfree[b] = [(s_v, s_v.n)]
                    pr.emit("scalar", lambda s, g=g, eh=eh, par=par, bi=bi: s.activation(
                        tE[g % 2][:, eh * 512:(eh + 1) * 512], t0b[bi], AF.Relu,
                        scale=wc[par][:, g:g + 1]),
                        waits=[(s_v, s_v.n)] + tE_free[g % 2] + t0_free[bi], inc=s_s,
                        drain=False)
                    t0_free[bi] = [(s_s, s_s.n)]
                tile_done = (s_s, s_s.n)
                pr.burst(q_st)
                pr.dma("gpsimd", hE_h[e * CAP + g * P: e * CAP + (g + 1) * P, :],
                       tE[g % 2], waits=[tile_done], inc=q_st)
                tE_free[g % 2] = [(q_st, q_st.n)]
            wt_free[par] = [(s_t, s_t.n)]
            eb_free[par] = [(s_t, s_t.n)]
            bbc_free[par] = [(s_v, s_v.n)]
            G_free[par] = [(s_t, s_t.n)]
            wc_free[par] = [(s_s, s_s.n)]
            idx_free[par] = [(q_ix, q_ix.n)]
            GT_free = [(s_t, s_t.n)]
        p3b_end = snap()
        hE_done = (q_st, q_st.n)
        adr_all = (q_st, q_st.n)

        region()
        xcb = [view(alloc(D), D) for _ in range(2)]
        hga = [view(alloc(D), D) for _ in range(2)]
        hgb = [view(alloc(D), D) for _ in range(2)]
        yt = [view(alloc(D), D) for _ in range(2)]
        xcb_free = [p3b_end, p3b_end]
        hg_free = [p3b_end, p3b_end]
        yt_free = [[], []]
        for tp in range(NT // 2):
            ts = (2 * tp, 2 * tp + 1)
            pr.burst(q_ld)
            for par, t in enumerate(ts):
                pr.dma("sync", xcb[par], x2_h[t * P:(t + 1) * P, :],
                       waits=xcb_free[par], inc=q_ld)
                pr.dma("sync", IX[:, 24 + par * 2: 26 + par * 2],
                       adr_h[t * P:(t + 1) * P, :],
                       waits=xcb_free[par] + [adr_all], inc=q_ld)
            c_ready = (q_ld, q_ld.n)
            pr.burst(q_ix)
            for par in range(2):
                for (dst, col) in ((hga, 24 + par * 2), (hgb, 25 + par * 2)):
                    pr.emit("gpsimd", lambda gp, dst=dst, col=col, par=par: gp.indirect_dma_start(
                        dst[par], None, hE_h,
                        bass.IndirectOffsetOnAxis(ap=IX[:, col:col + 1], axis=0)),
                        waits=[c_ready, hE_done, (q_ix, q_ix.mark)] + hg_free[par],
                        inc=q_ix, amt=16)
            g_ready2 = (q_ix, q_ix.n)
            pr.burst(q_st)
            for par, t in enumerate(ts):
                pr.emit("vector", lambda v, par=par: v.tensor_tensor(
                    yt[par], xcb[par], hga[par], ALU.add),
                    waits=[g_ready2] + yt_free[par], inc=s_v)
                pr.emit("vector", lambda v, par=par: v.tensor_tensor(
                    yt[par], yt[par], hgb[par], ALU.add), inc=s_v)
                xcb_free[par] = [(s_v, s_v.n)]
                hg_free[par] = [(s_v, s_v.n)]
                pr.dma("gpsimd", y_h[t * P:(t + 1) * P, :], yt[par],
                       waits=[(s_v, s_v.n)], inc=q_st)
            ytok = [(q_st, q_st.n)]
            yt_free[0] = ytok
            yt_free[1] = ytok

    for sem in (q_st, q_ix, q_sc):
        if sem.n > 0:
            pr.emit("gpsimd", lambda g, sem=sem: g.wait_ge(sem.h, sem.n))
            pr.emit("sync", lambda s, sem=sem: s.wait_ge(sem.h, sem.n))

    with nc.Block() as block:
        @block.sync
        def _(sync):
            pr.replay("sync", sync)

        @block.tensor
        def _(tensor):
            pr.replay("tensor", tensor)

        @block.vector
        def _(vector):
            pr.replay("vector", vector)

        @block.scalar
        def _(scalar):
            pr.replay("scalar", scalar)

        @block.gpsimd
        def _(gpsimd):
            pr.replay("gpsimd", gpsimd)

    return nc


def make_in_map(xb, wq, wk, wv, gw, ew, eb):
    return {
        "x": np.ascontiguousarray(xb),
        "wq": np.ascontiguousarray(wq),
        "wk": np.ascontiguousarray(wk),
        "wv": np.ascontiguousarray(wv),
        "gw": np.ascontiguousarray(gw),
        "ew": np.ascontiguousarray(ew),
        "eb": np.ascontiguousarray(eb),
        "ident": np.eye(P, dtype=np.float32),
        "onesr": np.ones((1, P), dtype=np.float32),
        "ltri": np.triu(np.ones((P, P), dtype=np.float32), 1),
        "onesc": np.ones((P, 1), dtype=np.float32),
        "iotaE": np.broadcast_to(np.arange(E, dtype=np.float32), (P, E)).copy(),
        "iotaEB": np.broadcast_to(np.arange(E, dtype=np.float32) + BIGC, (P, E)).copy(),
        "iota128": np.arange(P, dtype=np.float32).reshape(P, 1),
    }


def kernel(L0_attn_x, L0_attn_W_query, L0_attn_W_key, L0_attn_W_value,
           L0_ffn_gate_w, L0_ffn_expert_w, L0_ffn_expert_b):
    x = np.ascontiguousarray(np.asarray(L0_attn_x, dtype=np.float32))
    wq = np.asarray(L0_attn_W_query, dtype=np.float32)
    wk = np.asarray(L0_attn_W_key, dtype=np.float32)
    wv = np.asarray(L0_attn_W_value, dtype=np.float32)
    gw = np.asarray(L0_ffn_gate_w, dtype=np.float32)
    ew = np.asarray(L0_ffn_expert_w, dtype=np.float32)
    eb = np.asarray(L0_ffn_expert_b, dtype=np.float32)

    nc = build()
    in_maps = [make_in_map(x[b], wq, wk, wv, gw, ew, eb) for b in range(8)]
    res = run_bass_kernel_spmd(nc, in_maps, core_ids=list(range(8)))
    y = np.stack([np.asarray(res.results[b]["y"]) for b in range(8)], axis=0)
    return y.astype(np.float32)
